# version 02
# baseline (speedup 1.0000x reference)
import numpy as np
import ml_dtypes
import concourse.bass as bass
import concourse.mybir as mybir
from concourse.bass_utils import run_bass_kernel_spmd
from contextlib import ExitStack

F32 = mybir.dt.float32
BF16 = mybir.dt.bfloat16
U8 = mybir.dt.uint8
ALU = mybir.AluOpType
AF = mybir.ActivationFunctionType
AXX = mybir.AxisListType.X

LN2 = float(np.log(2.0))
RLN2 = float(1.0 / np.log(2.0))
P23 = float(2.0 ** 23)
P52 = float(2.0 ** 52)

D = 1024
DI = 2048
TOK = 128
NKD = D // 128
NCH = DI // 128
N_CORES = 8

ENGINES = ("sync", "tensor", "vector", "scalar", "gpsimd")


class Sched:

    def __init__(self):
        self.plan = {e: [] for e in ENGINES}
        self.counts = {e: 0 for e in ENGINES}
        self.last_eng = None

    def seq(self, eng, fn):
        deps = []
        if self.last_eng is not None and self.last_eng != eng:
            le = self.last_eng
            fn0, deps0, sig0 = self.plan[le][-1]
            if sig0 == 0:
                amt = 16 if le == "sync" else 1
                self.plan[le][-1] = (fn0, deps0, amt)
                self.counts[le] += amt
            deps.append((le, self.counts[le]))
        if eng in ("vector", "scalar", "gpsimd"):
            if self.counts[eng] > 0:
                deps.append((eng, self.counts[eng]))
            self.plan[eng].append((fn, tuple(deps), 1))
            self.counts[eng] += 1
        elif eng == "sync":
            self.plan[eng].append((fn, tuple(deps), 16))
            self.counts["sync"] += 16
        else:
            self.plan[eng].append((fn, tuple(deps), 0))
        self.last_eng = eng

    def emit(self, engine_obj, eng_name, sems):
        for fn, deps, sig in self.plan[eng_name]:
            for (e2, vv) in deps:
                engine_obj.wait_ge(sems[e2], vv)
            inst = fn(engine_obj)
            if sig:
                inst.then_inc(sems[eng_name], sig)


def build_nc(n_tiles=64, seq_tiles=16, debug=False):
    nc = bass.Bass("TRN2", target_bir_lowering=False)
    n_rows = n_tiles * TOK

    def din(name, shape, dt_):
        return nc.dram_tensor(name, shape, dt_, kind="ExternalInput").ap()

    x_h = din("x", [n_rows, D], F32)
    gamma_h = din("gamma", [1, D], F32)
    ss_h = din("step_size", [1, 1], F32)
    w_in_h = din("w_in", [2 * DI, D], F32)
    conv_w_h = din("conv_w", [DI, 4], F32)
    conv_b_h = din("conv_b", [DI, 1], F32)
    w_x_h = din("w_x", [96, DI], F32)
    w_dt_h = din("w_dt", [DI, 64], F32)
    b_dt_h = din("b_dt", [DI, 1], F32)
    w_out_h = din("w_out", [D, DI], F32)
    kern_h = din("ffn_kernel", [D, D], BF16)
    s_in_h = din("ffn_input_scale", [1, 1], F32)
    s_k_h = din("ffn_kernel_scale", [1, 1], F32)
    fbias_h = din("ffn_bias", [1, D], BF16)
    out_h = nc.dram_tensor("out", [n_rows, D], F32, kind="ExternalOutput").ap()
    dbg = {}
    if debug:
        for nm, w_, dt_ in [("xn", D, F32), ("xnb", D, BF16), ("x2", D, F32),
                            ("xqb", D, BF16), ("g", D, F32), ("xcp", 16 * 131, F32),
                            ("z", DI, F32), ("xc", DI, F32), ("dlb", 128, F32),
                            ("y", DI, BF16), ("sres", D, F32), ("win0", 2 * DI, BF16),
                            ("kq0", D, BF16), ("wdt", DI, BF16), ("wxt", 16 * 96, BF16),
                            ("sc", 32, F32), ("t11", 8, F32), ("cols", 16, F32),
                            ("qe", D, F32), ("qp2", D, F32), ("qr", D, F32),
                            ("qq2", D, F32), ("qm", D, F32), ("qs", D, F32),
                            ("qf", D, F32), ("qu", D, F32)]:
            dbg[nm] = nc.dram_tensor("dbg_" + nm, [128, w_], dt_,
                                     kind="ExternalOutput").ap()

    S = Sched()
    with ExitStack() as ctx:
        def sb(name, shape, dt_):
            return ctx.enter_context(nc.sbuf_tensor(name, shape, dt_))

        def psf(name, shape, dt_):
            return ctx.enter_context(nc.psum_tensor(name, shape, dt_))

        WIN = [sb(f"win{k}", [128, 2 * DI], BF16) for k in range(NKD)]
        WOUT = [sb(f"wout{k}", [128, D], BF16) for k in range(NCH)]
        KQ = [sb(f"kq{k}", [128, D], BF16) for k in range(NKD)]
        WXT = sb("wxt", [128, 16 * 96], BF16)
        WDT = sb("wdt", [64, DI], BF16)
        GAM = sb("gam", [128, D], F32)
        FBB = sb("fbb", [128, D], F32)
        CW = sb("cw", [128, 64], F32)
        CBIA = sb("cbia", [128, 16], F32)
        BDT = sb("bdt", [128, 16], F32)
        SC = sb("scpool", [128, 32], F32)
        ONE1 = sb("one1", [1, 128], F32)
        ONEC = sb("onec", [128, 1], F32)
        IDT = sb("idt", [128, 128], BF16)
        T11 = sb("t11", [1, 8], F32)

        (C_SS, C_SWIN, C_SX, C_SDT, C_SWOUT, C_RSIN, C_RSK, C_CFFN,
         C_RSW, C_RSX, C_RSDT, C_RSWOUT, C_EPS, C_ONE, C_NEG1,
         C_mLN2, C_LN2, C_7LN2, C_m7LN2, C_15LN2, C_8LN2, C_6LN2) = range(22)

        XT = sb("xt", [128, D], F32)
        XN = sb("xn", [128, D], F32)
        SA = sb("sa", [128, D], F32)
        SB_ = sb("sbb", [128, D], F32)
        SD = sb("sd", [128, D], F32)
        SG = sb("sg", [128, D], F32)
        GBUF = sb("gbuf", [128, D], F32)
        SRES = sb("sres", [128, D], F32)
        MSKA = sb("mska", [128, D], U8)
        XNB = sb("xnb", [128, D], BF16)
        XQB = sb("xqb", [128, D], BF16)
        XNB_FM = sb("xnbfm", [128, D], BF16)
        XQ_FM = sb("xqfm", [128, D], BF16)
        COLS = sb("colsp", [128, 16], F32)
        XCP = sb("xcp", [128, 16 * 131], F32)
        HALO = sb("halo", [128, 48], F32)
        Z = sb("zbuf", [128, DI], F32)
        XC = sb("xcbuf", [128, DI], F32)
        XCB = sb("xcbb", [128, DI], BF16)
        DLB = sb("dlb", [64, 128], BF16)
        Y = sb("ybuf", [128, DI], BF16)
        STATE = sb("state", [128, 16], F32)
        T1 = sb("tt1", [128, 128], F32)
        T2 = sb("tt2", [128, 128], F32)
        T3 = sb("tt3", [128, 128], F32)
        T4 = sb("tt4", [128, 128], F32)
        T5 = sb("tt5", [128, 128], F32)
        TMSK = sb("tmsk", [128, 128], U8)

        Q1 = XC[:, 0:D]
        Q2 = XC[:, D:2 * D]
        Q3 = Z[:, 0:D]
        SE = Z[:, D:2 * D]
        GROW = SA[0:1, 0:D]
        FBROWF = SB_[0:1, 0:D]
        FBROW = XNB[0:1, 0:D]

        PT = psf("pt", [128, 128], BF16)
        PA = psf("pa", [128, 128], F32)
        PB = psf("pb", [128, 128], F32)
        PF0 = psf("pf0", [128, 512], F32)
        PF1 = psf("pf1", [128, 512], F32)
        PX = psf("px", [96, 128], F32)
        PD = psf("pd", [128, 128], F32)

        sems = {e: ctx.enter_context(nc.semaphore(f"sem_{e}")) for e in ENGINES}
        block = ctx.enter_context(nc.Block())

        col = lambda tile, i: tile[:, i:i + 1]
        sq = S.seq

        def dma(dst, src):
            sq("sync", lambda e, d=dst, s=src: e.dma_start(out=d, in_=s))

        def dma_slow(dst, src):
            sq("sync", lambda e, d=dst, s=src: e.dma_start(
                out=d, in_=s, allow_slow_non_contiguous=True))

        def V(fn):
            sq("vector", fn)

        def A(fn):
            sq("scalar", fn)

        def T(fn):
            sq("tensor", fn)

        def G(fn):
            sq("gpsimd", fn)

        def rne22(src, t1, t2, out, msk=None):
            if msk is None:
                msk = MSKA[:]
            V(lambda e: e.tensor_scalar(t1, src, P23, P23, ALU.add, ALU.subtract))
            V(lambda e: e.tensor_scalar(t2, t1, P52, P52, ALU.add, ALU.subtract))
            V(lambda e: e.tensor_scalar(msk, t2, 0.0, None, ALU.is_equal))
            V(lambda e: e.select(out, msk, t1, t2))

        def f32ize(w32, t2, out, msk=None):
            if msk is None:
                msk = MSKA[:]
            V(lambda e: e.tensor_scalar(t2, w32, P52, P52, ALU.add, ALU.subtract))
            V(lambda e: e.tensor_scalar(msk, t2, 0.0, None, ALU.is_equal))
            V(lambda e: e.select(out, msk, w32, t2))

        G(lambda e: e.memset(ONE1[:], 1.0))
        G(lambda e: e.memset(ONEC[:], 1.0))
        G(lambda e: e.memset(IDT[:], 0.0))
        G(lambda e: e.affine_select(
            out=IDT[:], in_=IDT[:], compare_op=ALU.not_equal, fill=1.0,
            base=0, pattern=[[-1, 128]], channel_multiplier=1))
        for cc, val in [(C_EPS, 1e-5), (C_ONE, 1.0), (C_NEG1, -1.0),
                        (C_mLN2, -LN2), (C_LN2, LN2), (C_7LN2, 7 * LN2),
                        (C_m7LN2, -7 * LN2), (C_15LN2, 15 * LN2),
                        (C_8LN2, 8 * LN2), (C_6LN2, 6 * LN2)]:
            G(lambda e, c=cc, v=val: e.memset(col(SC, c), v))

        def bcast11(src11, dst_col):
            T(lambda e, s=src11, : e.matmul(PF0[:, 0:1], ONE1[:], s,
                                            start=True, stop=True))
            V(lambda e, d=dst_col: e.tensor_copy(d, PF0[:, 0:1]))

        def xpart_sum(col_ap, npart, dst11):
            T(lambda e, c=col_ap, n=npart: e.matmul(
                PD[0:1, 0:1], c, ONEC[0:n, :], start=True, stop=True))
            V(lambda e, d=dst11: e.tensor_copy(d, PD[0:1, 0:1]))

        dma(T11[0:1, 0:1], ss_h[:])
        dma(T11[0:1, 1:2], s_in_h[:])
        dma(T11[0:1, 2:3], s_k_h[:])
        V(lambda e: e.tensor_tensor(T11[0:1, 3:4], T11[0:1, 1:2], T11[0:1, 2:3], ALU.mult))
        V(lambda e: e.reciprocal(T11[0:1, 4:5], T11[0:1, 1:2]))
        V(lambda e: e.reciprocal(T11[0:1, 5:6], T11[0:1, 2:3]))
        bcast11(T11[0:1, 0:1], col(SC, C_SS))
        bcast11(T11[0:1, 3:4], col(SC, C_CFFN))
        bcast11(T11[0:1, 4:5], col(SC, C_RSIN))
        bcast11(T11[0:1, 5:6], col(SC, C_RSK))

        dma(GROW, gamma_h[:])
        for h in range(2):
            T(lambda e, h=h: e.matmul(PF0[:, 0:512], ONE1[:],
                                      SA[0:1, h * 512:(h + 1) * 512],
                                      start=True, stop=True))
            V(lambda e, h=h: e.tensor_copy(GAM[:, h * 512:(h + 1) * 512], PF0[:, 0:512]))
        dma(FBROW, fbias_h[:])
        V(lambda e: e.tensor_copy(FBROWF, FBROW))
        for h in range(2):
            T(lambda e, h=h: e.matmul(PF0[:, 0:512], ONE1[:],
                                      SB_[0:1, h * 512:(h + 1) * 512],
                                      start=True, stop=True))
            V(lambda e, h=h: e.tensor_copy(FBB[:, h * 512:(h + 1) * 512], PF0[:, 0:512]))

        for c in range(16):
            dma(CW[:, c * 4:(c + 1) * 4], conv_w_h[c * 128:(c + 1) * 128, :])
            dma(col(CBIA, c), conv_b_h[c * 128:(c + 1) * 128, :])
            dma(col(BDT, c), b_dt_h[c * 128:(c + 1) * 128, :])

        def quantize_weight(dram, nchunks, width, rsw_col, s_dst_col, rs_dst_col,
                            nelem, chunkbuf, sgnbuf, mskbuf, qbuf_bf16,
                            after_quant_chunk):
            for c in range(nchunks):
                dma(chunkbuf[:, 0:width], dram[c * 128:(c + 1) * 128, :])
                V(lambda e, c=c: e.reduce_sum(col(COLS, 0), chunkbuf[:, 0:width],
                                              AXX, apply_absolute_value=True))
                if c == 0:
                    V(lambda e: e.tensor_copy(col(COLS, 1), col(COLS, 0)))
                else:
                    V(lambda e: e.tensor_tensor(col(COLS, 1), col(COLS, 1),
                                                col(COLS, 0), ALU.add))
            xpart_sum(col(COLS, 1), 128, T11[0:1, 6:7])
            V(lambda e, n=nelem: e.tensor_scalar(T11[0:1, 6:7], T11[0:1, 6:7],
                                                 1.0 / n, 1e-8, ALU.mult, ALU.add))
            V(lambda e: e.reciprocal(T11[0:1, 7:8], T11[0:1, 6:7]))
            bcast11(T11[0:1, 6:7], s_dst_col)
            bcast11(T11[0:1, 7:8], rs_dst_col)
            for c in range(nchunks):
                dma(chunkbuf[:, 0:width], dram[c * 128:(c + 1) * 128, :])
                A(lambda e: e.activation(sgnbuf[:, 0:width], chunkbuf[:, 0:width], AF.Sign))
                V(lambda e, rc=rsw_col: e.tensor_scalar(
                    mskbuf[:, 0:width], chunkbuf[:, 0:width], rc, None, ALU.mult))
                A(lambda e: e.activation(mskbuf[:, 0:width], mskbuf[:, 0:width], AF.Abs))
                V(lambda e: e.tensor_scalar(mskbuf[:, 0:width], mskbuf[:, 0:width],
                                            0.5, None, ALU.is_gt))
                V(lambda e: e.tensor_tensor(qbuf_bf16[:, 0:width], sgnbuf[:, 0:width],
                                            mskbuf[:, 0:width], ALU.mult))
                after_quant_chunk(c)

        def win_chunk(c):
            for k in range(NKD):
                T(lambda e, k=k: e.transpose(PT[:], XNB[:, k * 128:(k + 1) * 128], IDT[:]))
                V(lambda e, k=k, c=c: e.tensor_copy(
                    WIN[k][:, c * 128:(c + 1) * 128], PT[:]))
        quantize_weight(w_in_h, 32, D, col(SC, C_RSW), col(SC, C_SWIN),
                        col(SC, C_RSW), 4096 * 1024, XT, SA, SB_, XNB, win_chunk)

        def wout_chunk(c):
            for k in range(NCH):
                T(lambda e, k=k: e.transpose(PT[:], XCB[:, k * 128:(k + 1) * 128], IDT[:]))
                V(lambda e, k=k, c=c: e.tensor_copy(
                    WOUT[k][:, c * 128:(c + 1) * 128], PT[:]))
        quantize_weight(w_out_h, 8, DI, col(SC, C_RSWOUT), col(SC, C_SWOUT),
                        col(SC, C_RSWOUT), 1024 * 2048, Z, XCP[:, 0:DI],
                        XC[:, 0:DI], XCB, wout_chunk)

        def wdt_chunk(c):
            T(lambda e: e.transpose(PT[0:64, :], XNB[:, 0:64], IDT[:]))
            V(lambda e, c=c: e.tensor_copy(WDT[:, c * 128:(c + 1) * 128], PT[0:64, :]))
        quantize_weight(w_dt_h, 16, 64, col(SC, C_RSDT), col(SC, C_SDT),
                        col(SC, C_RSDT), 2048 * 64, XT, SA, SB_, XNB, wdt_chunk)

        dma(XCP[0:96, 0:DI], w_x_h[:])
        V(lambda e: e.reduce_sum(COLS[0:96, 0:1], XCP[0:96, 0:DI], AXX,
                                 apply_absolute_value=True))
        xpart_sum(COLS[0:96, 0:1], 96, T11[0:1, 6:7])
        V(lambda e: e.tensor_scalar(T11[0:1, 6:7], T11[0:1, 6:7],
                                    1.0 / (96 * 2048), 1e-8, ALU.mult, ALU.add))
        V(lambda e: e.reciprocal(T11[0:1, 7:8], T11[0:1, 6:7]))
        bcast11(T11[0:1, 6:7], col(SC, C_SX))
        bcast11(T11[0:1, 7:8], col(SC, C_RSX))
        A(lambda e: e.activation(Z[0:96, 0:DI], XCP[0:96, 0:DI], AF.Sign))
        V(lambda e: e.tensor_scalar(XC[0:96, 0:DI], XCP[0:96, 0:DI],
                                    SC[0:96, C_RSX:C_RSX + 1], None, ALU.mult))
        A(lambda e: e.activation(XC[0:96, 0:DI], XC[0:96, 0:DI], AF.Abs))
        V(lambda e: e.tensor_scalar(XC[0:96, 0:DI], XC[0:96, 0:DI], 0.5, None, ALU.is_gt))
        V(lambda e: e.tensor_tensor(XCB[0:96, 0:DI], Z[0:96, 0:DI],
                                    XC[0:96, 0:DI], ALU.mult))
        for c in range(16):
            T(lambda e, c=c: e.transpose(PT[:, 0:96], XCB[0:96, c * 128:(c + 1) * 128],
                                         IDT[0:96, 0:96]))
            V(lambda e, c=c: e.tensor_copy(WXT[:, c * 96:(c + 1) * 96], PT[:, 0:96]))

        def quant_fp8(vbuf, outbuf_bf16, width):
            w_ = lambda t: t[:, 0:width]
            A(lambda e: e.activation(w_(SD), w_(vbuf), AF.Abs))
            V(lambda e: e.tensor_scalar_max(w_(SD), w_(SD), 1e-20))
            A(lambda e: e.activation(w_(SE), w_(SD), AF.Ln))
            V(lambda e: e.tensor_scalar(w_(SE), w_(SE), RLN2, 255.5, ALU.mult, ALU.add))
            rne22(w_(SE), w_(SD), w_(SG), w_(SE), w_(MSKA))
            V(lambda e: e.tensor_scalar(w_(SE), w_(SE), -256.0, -6.0, ALU.add, ALU.max))
            V(lambda e: e.tensor_scalar_min(w_(SE), w_(SE), 8.0))
            A(lambda e: e.activation(w_(SD), w_(SE), AF.Exp,
                                     bias=col(SC, C_8LN2), scale=col(SC, C_mLN2)))
            rne22(w_(SD), w_(SG), w_(SRES), w_(SD), w_(MSKA))
            V(lambda e: e.tensor_tensor(w_(SG), w_(vbuf), w_(SD), ALU.mult))
            V(lambda e: e.tensor_scalar(w_(SG), w_(SG), float(2.0 ** 15), None, ALU.mult))
            f32ize(w_(SG), w_(SRES), w_(GBUF), w_(MSKA))
            V(lambda e: e.tensor_scalar(w_(GBUF), w_(GBUF), float(2.0 ** -20), 1024.0,
                                        ALU.mult, ALU.add))
            rne22(w_(GBUF), w_(SG), w_(SRES), w_(GBUF), w_(MSKA))
            A(lambda e: e.activation(w_(SD), w_(SE), AF.Exp,
                                     bias=col(SC, C_6LN2), scale=col(SC, C_LN2)))
            rne22(w_(SD), w_(SE), w_(SRES), w_(SD), w_(MSKA))
            V(lambda e: e.tensor_scalar(w_(SD), w_(SD), float(2.0 ** -9), None, ALU.mult))
            V(lambda e: e.scalar_tensor_tensor(w_(SRES), w_(GBUF), -1024.0, w_(SD),
                                               ALU.add, ALU.mult))
            V(lambda e: e.tensor_copy(w_(outbuf_bf16), w_(SRES)))

        for k in range(NKD):
            dma(XQB[:], kern_h[k * 128:(k + 1) * 128, :])
            V(lambda e: e.tensor_copy(SA[:], XQB[:]))
            V(lambda e: e.tensor_scalar(SA[:], SA[:], col(SC, C_RSK), None, ALU.mult))
            quant_fp8(SA, XQB, D)
            V(lambda e, k=k: e.tensor_copy(KQ[k][:], XQB[:]))

        def sp_chain(xin, dst, ta, tb, tc):
            A(lambda e: e.activation(ta, xin, AF.Square))
            V(lambda e: e.tensor_scalar(ta, ta, 4.0, 1e6, ALU.add, ALU.min))
            V(lambda e: e.tensor_scalar(tb, ta, 16.0, -0.375, ALU.is_gt, ALU.mult))
            V(lambda e: e.tensor_scalar(tc, ta, 64.0, -0.0625, ALU.is_gt, ALU.mult))
            V(lambda e: e.scalar_tensor_tensor(tb, tb, 0.5, tc, ALU.add, ALU.add))
            V(lambda e: e.tensor_scalar(tc, ta, 256.0, -0.03125, ALU.is_gt, ALU.mult))
            V(lambda e: e.tensor_tensor(tb, tb, tc, ALU.add))
            for it in range(3):
                V(lambda e: e.tensor_tensor(tc, tb, tb, ALU.mult))
                V(lambda e: e.scalar_tensor_tensor(tc, tc, -0.5, ta, ALU.mult, ALU.mult))
                V(lambda e: e.scalar_tensor_tensor(tb, tc, 1.5, tb, ALU.add, ALU.mult))
                if it == 0:
                    V(lambda e: e.tensor_scalar_max(tb, tb, 1e-6))
            V(lambda e: e.scalar_tensor_tensor(tc, ta, 0.5, tb, ALU.mult, ALU.mult))
            V(lambda e: e.scalar_tensor_tensor(dst, xin, 0.5, tc, ALU.mult, ALU.add))

        def soft_bf16_x2(src, dst):
            A(lambda e: e.activation(SA[:], src, AF.Abs))
            V(lambda e: e.tensor_scalar_max(SA[:], SA[:], 1e-20))
            A(lambda e: e.activation(SB_[:], SA[:], AF.Ln))
            V(lambda e: e.tensor_scalar(SB_[:], SB_[:], RLN2, 255.5, ALU.mult, ALU.add))
            rne22(SB_[:], SD[:], SA[:], SE)
            V(lambda e: e.tensor_scalar(SE, SE, -256.0, -15.0, ALU.add, ALU.max))
            V(lambda e: e.tensor_scalar_min(SE, SE, 6.0))
            if debug:
                dma(dbg["qe"], SE)
            A(lambda e: e.activation(SA[:], SE, AF.Exp,
                                     bias=col(SC, C_mLN2), scale=col(SC, C_mLN2)))
            rne22(SA[:], SB_[:], SD[:], SA[:])
            A(lambda e: e.activation(SB_[:], SE, AF.Exp,
                                     bias=col(SC, C_7LN2), scale=col(SC, C_mLN2)))
            rne22(SB_[:], SD[:], Q1, SB_[:])
            V(lambda e: e.tensor_scalar(MSKA[:], SE, -7.5, None, ALU.is_ge))
            V(lambda e: e.tensor_scalar(SA[:], SA[:], 256.0, None, ALU.mult))
            V(lambda e: e.select(SD[:], MSKA[:], SB_[:], SA[:]))
            if debug:
                dma(dbg["qp2"], SD[:])
            V(lambda e: e.tensor_tensor(Q1, src, SD[:], ALU.mult))
            if debug:
                dma(dbg["qm"], Q1)
            V(lambda e: e.tensor_scalar(Q1, Q1, 65536.0, None, ALU.mult))
            if debug:
                dma(dbg["qs"], Q1)
            f32ize(Q1, Q2, Q3)
            if debug:
                dma(dbg["qf"], Q3)
            V(lambda e: e.tensor_scalar(Q3, Q3, float(2.0 ** -16), 1024.0,
                                        ALU.mult, ALU.add))
            if debug:
                dma(dbg["qu"], Q3)
            rne22(Q3, Q1, Q2, Q3)
            if debug:
                dma(dbg["qr"], Q1)
            A(lambda e: e.activation(SA[:], SE, AF.Exp,
                                     bias=col(SC, C_15LN2), scale=col(SC, C_LN2)))
            rne22(SA[:], SB_[:], SD[:], SA[:])
            V(lambda e: e.tensor_scalar(SA[:], SA[:], float(2.0 ** -22), None, ALU.mult))
            A(lambda e: e.activation(SB_[:], SE, AF.Exp,
                                     bias=col(SC, C_7LN2), scale=col(SC, C_LN2)))
            rne22(SB_[:], SD[:], Q2, SB_[:])
            V(lambda e: e.tensor_scalar(SB_[:], SB_[:], float(2.0 ** -14), None, ALU.mult))
            V(lambda e: e.tensor_scalar(MSKA[:], SE, -1.5, None, ALU.is_ge))
            V(lambda e: e.select(SD[:], MSKA[:], SB_[:], SA[:]))
            if debug:
                dma(dbg["qq2"], SD[:])
            V(lambda e: e.scalar_tensor_tensor(dst, Q3, -1024.0, SD[:],
                                               ALU.add, ALU.mult))

        def soft_bf16_wide(src, dst, ta, tb, tc, td, te):
            A(lambda e: e.activation(ta, src, AF.Abs))
            V(lambda e: e.tensor_scalar_max(ta, ta, 1e-30))
            A(lambda e: e.activation(tb, ta, AF.Ln))
            V(lambda e: e.tensor_scalar(tb, tb, RLN2, 255.5, ALU.mult, ALU.add))
            rne22(tb, ta, tc, tb)
            V(lambda e: e.tensor_scalar(tb, tb, -256.0, -30.0, ALU.add, ALU.max))
            V(lambda e: e.tensor_scalar_min(tb, tb, 12.0))
            A(lambda e: e.activation(ta, tb, AF.Exp,
                                     bias=col(SC, C_7LN2), scale=col(SC, C_mLN2)))
            V(lambda e: e.tensor_tensor(tc, src, ta, ALU.mult))
            V(lambda e: e.tensor_scalar(tc, tc, 65536.0, None, ALU.mult))
            f32ize(tc, td, te)
            V(lambda e: e.tensor_scalar(te, te, float(2.0 ** -16), 1024.0,
                                        ALU.mult, ALU.add))
            rne22(te, tc, td, te)
            A(lambda e: e.activation(ta, tb, AF.Exp,
                                     bias=col(SC, C_m7LN2), scale=col(SC, C_LN2)))
            V(lambda e: e.scalar_tensor_tensor(dst, te, -1024.0, ta,
                                               ALU.add, ALU.mult))

        if debug:
            dma(dbg["sc"], SC[:])
            dma(dbg["t11"][0:1, :], T11[:])
            dma(dbg["cols"], COLS[:])
            dma(dbg["win0"], WIN[0][:])
            dma(dbg["kq0"], KQ[0][:])
            dma(dbg["wdt"][0:64, :], WDT[:])
            dma(dbg["wxt"], WXT[:])
        xcp3 = XCP[:].rearrange("p (c w) -> p c w", w=131)
        halo3 = HALO[:].rearrange("p (c w) -> p c w", w=3)

        for tile in range(n_tiles):
            seq_start = (tile % seq_tiles) == 0
            r0 = tile * TOK

            dma(XT[:], x_h[r0:r0 + TOK, :])

            A(lambda e: e.activation(SA[:], XT[:], AF.Square, accum_out=col(COLS, 0)))
            A(lambda e: e.activation(SA[:], XT[:], AF.Identity, accum_out=col(COLS, 1)))
            V(lambda e: e.tensor_scalar(col(COLS, 2), col(COLS, 1), 1.0 / D, None, ALU.mult))
            V(lambda e: e.tensor_scalar(col(COLS, 3), col(COLS, 0), 1.0 / D, None, ALU.mult))
            V(lambda e: e.tensor_tensor(col(COLS, 4), col(COLS, 2), col(COLS, 2), ALU.mult))
            V(lambda e: e.tensor_tensor(col(COLS, 5), col(COLS, 3), col(COLS, 4), ALU.subtract))
            A(lambda e: e.activation(col(COLS, 6), col(COLS, 5), AF.Sqrt,
                                     bias=col(SC, C_EPS)))
            V(lambda e: e.reciprocal(col(COLS, 7), col(COLS, 6)))
            V(lambda e: e.tensor_scalar(XN[:], XT[:], col(COLS, 2), col(COLS, 7),
                                        ALU.subtract, ALU.mult))

            A(lambda e: e.activation(SA[:], XN[:], AF.Identity, accum_out=col(COLS, 8)))
            V(lambda e: e.tensor_scalar(col(COLS, 9), col(COLS, 8), 1.0 / D, None, ALU.mult))
            V(lambda e: e.tensor_scalar(SA[:], XN[:], col(COLS, 9), None, ALU.subtract))
            A(lambda e: e.activation(SB_[:], SA[:], AF.Square, accum_out=col(COLS, 10)))
            V(lambda e: e.tensor_scalar(col(COLS, 11), col(COLS, 10), 1.0 / D, 1e-9,
                                        ALU.mult, ALU.add))
            V(lambda e: e.tensor_scalar(col(COLS, 12), col(COLS, 11), 0.0625, -2.0,
                                        ALU.is_ge, ALU.mult))
            V(lambda e: e.tensor_scalar(col(COLS, 12), col(COLS, 12), 4.0, None, ALU.add))
            for thr, dlt in [(0.25, -1.0), (4.0, -0.5), (16.0, -0.25), (64.0, -0.125),
                             (256.0, -0.0625), (1024.0, -0.03125), (4096.0, -0.015625),
                             (16384.0, -0.0078125), (65536.0, -0.00390625)]:
                V(lambda e, t_=thr, d_=dlt: e.tensor_scalar(
                    col(COLS, 13), col(COLS, 11), t_, d_, ALU.is_ge, ALU.mult))
                V(lambda e: e.tensor_tensor(col(COLS, 12), col(COLS, 12),
                                            col(COLS, 13), ALU.add))
            V(lambda e: e.tensor_tensor(col(COLS, 14), col(COLS, 12),
                                        col(SC, C_SS), ALU.mult))
            V(lambda e: e.tensor_scalar(SA[:], SA[:], col(COLS, 14), None, ALU.mult))
            V(lambda e: e.tensor_tensor(XNB[:], SA[:], GAM[:], ALU.mult))

            if debug and tile == 0:
                dma(dbg["xn"], XN[:])
                dma(dbg["xnb"], XNB[:])
            soft_bf16_x2(XN[:], SG[:])
            if debug and tile == 0:
                dma(dbg["x2"], SG[:])

            V(lambda e: e.tensor_scalar(SA[:], SG[:], col(SC, C_RSIN), None, ALU.mult))
            quant_fp8(SA, XQB, D)

            if debug and tile == 0:
                dma(dbg["xqb"], XQB[:])
            for k in range(NKD):
                T(lambda e, k=k: e.transpose(PT[:], XNB[:, k * 128:(k + 1) * 128], IDT[:]))
                V(lambda e, k=k: e.tensor_copy(XNB_FM[:, k * 128:(k + 1) * 128], PT[:]))
                T(lambda e, k=k: e.transpose(PT[:], XQB[:, k * 128:(k + 1) * 128], IDT[:]))
                V(lambda e, k=k: e.tensor_copy(XQ_FM[:, k * 128:(k + 1) * 128], PT[:]))

            for h, PF in ((0, PF0), (1, PF1)):
                for k in range(NKD):
                    T(lambda e, k=k, h=h, PF=PF: e.matmul(
                        PF[:], XQ_FM[:, k * 128:(k + 1) * 128],
                        KQ[k][:, h * 512:(h + 1) * 512],
                        start=(k == 0), stop=(k == NKD - 1)))

            for h, PF in ((0, PF0), (1, PF1)):
                V(lambda e, h=h, PF=PF: e.tensor_scalar(
                    SA[:, h * 512:(h + 1) * 512], PF[:], col(SC, C_CFFN), None, ALU.mult))
            soft_bf16_wide(SA[:], SD[:], SB_[:], SE, GBUF[:], SG[:], SRES[:])
            V(lambda e: e.tensor_tensor(GBUF[:], SD[:], FBB[:], ALU.add))
            A(lambda e: e.activation(SG[:], GBUF[:], AF.Silu))

            if debug and tile == 0:
                dma(dbg["g"], SG[:])
            if seq_start:
                V(lambda e: e.memset(xcp3[:, :, 0:3], 0.0))
                V(lambda e: e.memset(STATE[:], 0.0))
            else:
                V(lambda e: e.tensor_copy(xcp3[:, :, 0:3], halo3[:, :, :]))
            for m in range(32):
                PM = PA if (m % 2 == 0) else PB
                for k in range(NKD):
                    T(lambda e, m=m, k=k, PM=PM: e.matmul(
                        PM[:], WIN[k][:, m * 128:(m + 1) * 128],
                        XNB_FM[:, k * 128:(k + 1) * 128],
                        start=(k == 0), stop=(k == NKD - 1)))
                if m < 16:
                    V(lambda e, m=m, PM=PM: e.tensor_scalar(
                        XCP[:, m * 131 + 3:m * 131 + 131], PM[:],
                        col(SC, C_SWIN), None, ALU.mult))
                else:
                    V(lambda e, m=m, PM=PM: e.tensor_scalar(
                        Z[:, (m - 16) * 128:(m - 15) * 128], PM[:],
                        col(SC, C_SWIN), None, ALU.mult))
            V(lambda e: e.tensor_copy(halo3[:, :, :], xcp3[:, :, 128:131]))

            if debug and tile == 0:
                dma(dbg["xcp"], XCP[:])
                dma(dbg["z"], Z[:])
            for m in range(16):
                base = m * 131
                V(lambda e, b=base, m=m: e.tensor_scalar(
                    T1[:], XCP[:, b:b + 128], col(CW, 4 * m), None, ALU.mult))
                V(lambda e, b=base, m=m: e.scalar_tensor_tensor(
                    T2[:], XCP[:, b + 1:b + 129], col(CW, 4 * m + 1), T1[:],
                    ALU.mult, ALU.add))
                V(lambda e, b=base, m=m: e.tensor_scalar(
                    T1[:], XCP[:, b + 3:b + 131], col(CW, 4 * m + 3), col(CBIA, m),
                    ALU.mult, ALU.add))
                V(lambda e, b=base, m=m: e.scalar_tensor_tensor(
                    T1[:], XCP[:, b + 2:b + 130], col(CW, 4 * m + 2), T1[:],
                    ALU.mult, ALU.add))
                V(lambda e: e.tensor_tensor(T3[:], T2[:], T1[:], ALU.add))
                sp_chain(T3[:], XC[:, m * 128:(m + 1) * 128], T1[:], T2[:], T4[:])
            V(lambda e: e.tensor_copy(XCB[:], XC[:]))

            if debug and tile == 0:
                dma(dbg["xc"], XC[:])
            for c in range(16):
                T(lambda e, c=c: e.matmul(
                    PX[:], WXT[:, c * 96:(c + 1) * 96], XCB[:, c * 128:(c + 1) * 128],
                    start=(c == 0), stop=(c == 15)))
            V(lambda e: e.tensor_scalar(DLB[:], PX[0:64, :], SC[0:64, C_SX:C_SX + 1],
                                        None, ALU.mult))

            if debug and tile == 0:
                V(lambda e: e.tensor_copy(T1[0:64, :], DLB[:]))
                dma(dbg["dlb"][0:64, :], T1[0:64, :])
            for m in range(16):
                T(lambda e, m=m: e.matmul(PD[:], WDT[:, m * 128:(m + 1) * 128], DLB[:],
                                          start=True, stop=True))
                A(lambda e, m=m: e.activation(T1[:], PD[:], AF.Identity,
                                              bias=col(BDT, m), scale=col(SC, C_SDT)))
                A(lambda e: e.activation(T2[:], T1[:], AF.Abs))
                A(lambda e: e.activation(T3[:], T2[:], AF.Ln, bias=col(SC, C_ONE)))
                A(lambda e: e.activation(T4[:], T3[:], AF.Exp, scale=col(SC, C_NEG1)))
                V(lambda e: e.scalar_tensor_tensor(T3[:], T2[:], 1.0, T4[:],
                                                   ALU.add, ALU.mult))
                V(lambda e: e.tensor_scalar(T3[:], T3[:], -1.0, 2.0, ALU.mult, ALU.add))
                V(lambda e: e.tensor_tensor(T4[:], T4[:], T3[:], ALU.mult))
                V(lambda e: e.tensor_tensor(T2[:], T1[:], T4[:], ALU.mult))
                V(lambda e: e.tensor_scalar(T2[:], T2[:], 0.5, 0.5, ALU.mult, ALU.add))
                V(lambda e: e.tensor_scalar(T2[:], T2[:], 32768.0, 2.0, ALU.mult, ALU.add))
                rne22(T2[:], T1[:], T3[:], T2[:], TMSK[:])
                V(lambda e: e.tensor_scalar(T1[:], T2[:], float(2.0 ** -15),
                                            float(-2.0 ** -14), ALU.mult, ALU.add))
                V(lambda e: e.tensor_scalar(T3[:], T2[:], float(-2.0 ** -15),
                                            1.00006103515625, ALU.mult, ALU.add))
                V(lambda e, m=m: e.tensor_tensor(T2[:], T3[:],
                                                 XC[:, m * 128:(m + 1) * 128], ALU.mult))
                V(lambda e, m=m: e.tensor_tensor_scan(
                    T3[:], T1[:], T2[:], col(STATE, m), ALU.mult, ALU.add))
                V(lambda e, m=m: e.tensor_copy(col(STATE, m), T3[:, 127:128]))
                sp_chain(Z[:, m * 128:(m + 1) * 128], T1[:], T2[:], T4[:], T5[:])
                V(lambda e, m=m: e.tensor_tensor(Y[:, m * 128:(m + 1) * 128],
                                                 T3[:], T1[:], ALU.mult))

            if debug and tile == 0:
                dma(dbg["y"], Y[:])
            for h, PF in ((0, PF0), (1, PF1)):
                for k in range(NCH):
                    T(lambda e, k=k, h=h, PF=PF: e.matmul(
                        PF[:], Y[:, k * 128:(k + 1) * 128],
                        WOUT[k][:, h * 512:(h + 1) * 512],
                        start=(k == 0), stop=(k == NCH - 1)))
                V(lambda e, h=h, PF=PF: e.scalar_tensor_tensor(
                    SRES[:, h * 512:(h + 1) * 512], PF[:], col(SC, C_SWOUT),
                    XN[:, h * 512:(h + 1) * 512], ALU.mult, ALU.add))

            V(lambda e: e.tensor_tensor(SA[:], SRES[:], SG[:], ALU.mult))
            V(lambda e: e.tensor_tensor(SA[:], SA[:], XT[:], ALU.add))

            dma(out_h[r0:r0 + TOK, :], SA[:])
            if debug and tile == 0:
                dma(dbg["sres"], SRES[:])

        G(lambda e: e.engine_nop())

        @block.sync
        def _(sync):
            S.emit(sync, "sync", sems)

        @block.tensor
        def _(tensor):
            S.emit(tensor, "tensor", sems)

        @block.vector
        def _(vector):
            S.emit(vector, "vector", sems)

        @block.scalar
        def _(scalar):
            S.emit(scalar, "scalar", sems)

        @block.gpsimd
        def _(gpsimd):
            S.emit(gpsimd, "gpsimd", sems)

    return nc


_NC_CACHE = {}


def _get_nc(n_tiles, seq_tiles):
    key = (n_tiles, seq_tiles)
    if key not in _NC_CACHE:
        _NC_CACHE[key] = build_nc(n_tiles, seq_tiles)
    return _NC_CACHE[key]


def _in_map(x_shard, inputs):
    f32 = lambda a: np.ascontiguousarray(np.asarray(a), dtype=np.float32)
    return {
        "x": f32(x_shard).reshape(-1, D),
        "gamma": f32(inputs["L0_attn_gamma"]).reshape(1, D),
        "step_size": f32(inputs["L0_attn_step_size"]).reshape(1, 1),
        "w_in": f32(inputs["L0_attn_W_in"]),
        "conv_w": f32(inputs["L0_attn_conv_w"]).reshape(DI, 4),
        "conv_b": f32(inputs["L0_attn_conv_b"]).reshape(DI, 1),
        "w_x": f32(inputs["L0_attn_W_x"]),
        "w_dt": f32(inputs["L0_attn_W_dt"]),
        "b_dt": f32(inputs["L0_attn_b_dt"]).reshape(DI, 1),
        "w_out": f32(inputs["L0_attn_W_out"]),
        "ffn_kernel": np.ascontiguousarray(
            np.asarray(inputs["L0_ffn_kernel"]), dtype=ml_dtypes.bfloat16),
        "ffn_input_scale": f32(inputs["L0_ffn_input_scale"]).reshape(1, 1),
        "ffn_kernel_scale": f32(inputs["L0_ffn_kernel_scale"]).reshape(1, 1),
        "ffn_bias": np.ascontiguousarray(
            np.asarray(inputs["L0_ffn_bias"]), dtype=ml_dtypes.bfloat16).reshape(1, D),
    }


def kernel(**inputs):
    x = np.asarray(inputs["L0_attn_x"])
    B, L, _ = x.shape
    per = B // N_CORES
    nc = _get_nc((per * L) // TOK, L // TOK)
    in_maps = [_in_map(x[i * per:(i + 1) * per], inputs) for i in range(N_CORES)]
    res = run_bass_kernel_spmd(nc, in_maps, core_ids=list(range(N_CORES)))
    outs = [np.asarray(r["out"], dtype=np.float32).reshape(per, L, D)
            for r in res.results]
    return np.concatenate(outs, axis=0)


# version 03
# speedup vs baseline: 1.7174x; 1.7174x over previous
import numpy as np
import ml_dtypes
import concourse.bass as bass
import concourse.mybir as mybir
from concourse.bass_utils import run_bass_kernel_spmd
from contextlib import ExitStack

F32 = mybir.dt.float32
BF16 = mybir.dt.bfloat16
U8 = mybir.dt.uint8
ALU = mybir.AluOpType
AF = mybir.ActivationFunctionType
AXX = mybir.AxisListType.X

LN2 = float(np.log(2.0))
RLN2 = float(1.0 / np.log(2.0))
P23 = float(2.0 ** 23)
P52 = float(2.0 ** 52)

D = 1024
DI = 2048
TOK = 128
NKD = D // 128
NCH = DI // 128
N_CORES = 8

ENGINES = ("sync", "tensor", "vector", "scalar", "gpsimd")
VSG = ("vector", "scalar", "gpsimd")


def _ref_rne(i0, i1, s0, s1, imm2):
    t1 = (i0 + s0) - s0
    t2 = (t1 + s1) - s1
    return np.where(t2 == 0, t1, t2)


def _ref_f32z(i0, i1, s0, s1, imm2):
    w = i0 * imm2
    t2 = (w + s1) - s1
    return np.where(t2 == 0, w, t2)


def _ref_sqy(i0, i1, s0, s1, imm2):
    return np.minimum(i0 * i0 + 4.0, s0)


def _ref_sqr0a(i0, i1, s0, s1, imm2):
    return np.where(i0 > s0 * 4.0, s1 * imm2,
                    np.where(i0 > s0, s1, imm2)).astype(i0.dtype)


def _ref_sqr0b(i0, i1, s0, s1, imm2):
    return np.where(i0 > s0, np.asarray(s1, dtype=i0.dtype), i1)


def _ref_sqit1(i0, i1, s0, s1, imm2):
    return np.maximum(((i0 * i0) * i1 * s0 + s1) * i0, imm2)


def _ref_sqit(i0, i1, s0, s1, imm2):
    return ((i0 * i0) * i1 * s0 + s1) * i0


def _ref_dtq(i0, i1, s0, s1, imm2):
    av = np.maximum(i0, 0.0 - i0)
    t = (av + 1.0) * i1
    t = 2.0 - t
    t = i1 * t
    return i0 * t


def _ref_dtb(i0, i1, s0, s1, imm2):
    return (i0 * s0 + s1) * i1


_OPS = {}


def _register_custom_ops():
    if _OPS:
        return _OPS
    import concourse.dve_ops as dom
    from concourse.dve_op import DveOp
    from concourse.dve_spec import (Spec, Src0, Src1, C0, C1, C2, Zero, One,
                                    select, eq, sq, maxx, minn, lower)
    from concourse.dve_table_gen import dve_ver_for
    ver = dve_ver_for("TRN2")
    K2 = One + One
    K4 = K2 * K2

    def reg(name, body, ref, row, perf=False):
        existing = next((o for o in dom.OPS if o.name == name), None)
        if existing is not None:
            _OPS[name] = existing
            return
        spec = Spec(body=body)
        try:
            low = lower(spec, ver=ver, perf_en=perf, name=name)
        except Exception:
            perf = False
            low = lower(spec, ver=ver, perf_en=False, name=name)
        op = DveOp(name, spec, subdim=False, uops_sha={ver: low.sha(ver)},
                   perf_en={ver: perf} if perf else {}, reference=ref)
        dom.OPS.append(op)
        dom.CUSTOM_DVE_SPECS[name] = spec
        dom.CUSTOM_DVE_REFS[name] = ref
        dom._SUB_OPCODE_FOR_NAME[name] = row
        _OPS[name] = op

    t1 = (Src0 + C0) - C0
    t2 = (t1 + C1) - C1
    reg("ANT_RNE", select(eq(t2, Zero), t1, t2), _ref_rne, 1)
    w = Src0 * C2
    tw = (w + C1) - C1
    reg("ANT_F32Z", select(eq(tw, Zero), w, tw), _ref_f32z, 2)
    reg("ANT_SQY", minn(sq(Src0) + K4, C0), _ref_sqy, 3)
    reg("ANT_SQR0A",
        select(Src0 > C0 * K4, C1 * C2, select(Src0 > C0, C1, C2)),
        _ref_sqr0a, 4)
    reg("ANT_SQR0B", select(Src0 > C0, C1, Src1), _ref_sqr0b, 5)
    reg("ANT_SQIT1", maxx(((sq(Src0) * Src1) * C0 + C1) * Src0, C2),
        _ref_sqit1, 6)
    reg("ANT_SQIT", ((sq(Src0) * Src1) * C0 + C1) * Src0, _ref_sqit, 7)
    av = maxx(Src0, Zero - Src0)
    reg("ANT_DTQ", Src0 * (Src1 * (K2 - (av + One) * Src1)), _ref_dtq, 8)
    reg("ANT_DTB", (Src0 * C0 + C1) * Src1, _ref_dtb, 9)
    return _OPS


class Sched:

    def __init__(self):
        self.plan = {e: [] for e in ENGINES}
        self.counts = {e: 0 for e in ENGINES}
        self.last_eng = None

    def _ensure_sig(self, eng):
        fn0, deps0, sig0 = self.plan[eng][-1]
        if sig0 == 0:
            amt = 16 if eng == "sync" else 1
            self.plan[eng][-1] = (fn0, deps0, amt)
            self.counts[eng] += amt
        return (eng, self.counts[eng])

    def seq(self, eng, fn, after=(), lag=0, chain=True, sig=False):
        deps = {}
        for t in after:
            if t is None:
                continue
            e2, vv = t
            if e2 != eng or eng not in VSG:
                deps[e2] = max(deps.get(e2, 0), vv)
        if chain and self.last_eng is not None and self.last_eng != eng:
            le = self.last_eng
            tok = self._ensure_sig(le)
            deps[le] = max(deps.get(le, 0), tok[1])
        if eng in VSG:
            own = self.counts[eng] - lag
            for t in after:
                if t is not None and t[0] == eng:
                    own = max(own, t[1])
            if own > 0:
                deps[eng] = max(deps.get(eng, 0), own)
            self.plan[eng].append((fn, tuple(deps.items()), 1))
            self.counts[eng] += 1
        elif eng == "sync":
            self.plan[eng].append((fn, tuple(deps.items()), 16))
            self.counts["sync"] += 16
        else:
            amt = 1 if sig else 0
            self.plan[eng].append((fn, tuple(deps.items()), amt))
            self.counts[eng] += amt
        if chain:
            self.last_eng = eng
        return (eng, self.counts[eng])

    def now(self, eng):
        return (eng, self.counts[eng])

    def emit(self, engine_obj, eng_name, sems):
        for fn, deps, sig in self.plan[eng_name]:
            for (e2, vv) in deps:
                engine_obj.wait_ge(sems[e2], vv)
            inst = fn(engine_obj)
            if sig:
                inst.then_inc(sems[eng_name], sig)


def build_nc(n_tiles=64, seq_tiles=16, debug=False):
    nc = bass.Bass("TRN2", target_bir_lowering=False)
    n_rows = n_tiles * TOK

    def din(name, shape, dt_):
        return nc.dram_tensor(name, shape, dt_, kind="ExternalInput").ap()

    x_h = din("x", [n_rows, D], F32)
    gamma_h = din("gamma", [1, D], F32)
    ss_h = din("step_size", [1, 1], F32)
    w_in_h = din("w_in", [2 * DI, D], F32)
    conv_w_h = din("conv_w", [DI, 4], F32)
    conv_b_h = din("conv_b", [DI, 1], F32)
    w_x_h = din("w_x", [96, DI], F32)
    w_dt_h = din("w_dt", [DI, 64], F32)
    b_dt_h = din("b_dt", [DI, 1], F32)
    w_out_h = din("w_out", [D, DI], F32)
    kern_h = din("ffn_kernel", [D, D], BF16)
    s_in_h = din("ffn_input_scale", [1, 1], F32)
    s_k_h = din("ffn_kernel_scale", [1, 1], F32)
    fbias_h = din("ffn_bias", [1, D], BF16)
    out_h = nc.dram_tensor("out", [n_rows, D], F32, kind="ExternalOutput").ap()
    dbg = {}
    if debug:
        for nm, w_, dt_ in [("xn", D, F32), ("x2", D, F32), ("xqb", D, BF16),
                            ("g", D, F32), ("z", DI, F32), ("xc", DI, F32),
                            ("y", DI, BF16), ("sres", D, F32), ("xcp", 16 * 131, F32),
                            ("convo", D, F32)]:
            dbg[nm] = nc.dram_tensor("dbg_" + nm, [128, w_], dt_,
                                     kind="ExternalOutput").ap()

    S = Sched()
    with ExitStack() as ctx:
        def sb(name, shape, dt_):
            return ctx.enter_context(nc.sbuf_tensor(name, shape, dt_))

        def psf(name, shape, dt_):
            return ctx.enter_context(nc.psum_tensor(name, shape, dt_))

        WIN = [sb(f"win{k}", [128, 2 * DI], BF16) for k in range(NKD)]
        WOUT = [sb(f"wout{k}", [128, D], BF16) for k in range(NCH)]
        KQ = [sb(f"kq{k}", [128, D], BF16) for k in range(NKD)]
        WXT = sb("wxt", [128, 16 * 96], BF16)
        WDT = sb("wdt", [64, DI], BF16)
        GAM = sb("gam", [128, D], F32)
        FBB = sb("fbb", [128, D], F32)
        CW = sb("cw", [128, 64], F32)
        CBIA = sb("cbia", [128, 16], F32)
        BDT = sb("bdt", [128, 16], F32)
        SC = sb("scpool", [128, 32], F32)
        ONE1 = sb("one1", [1, 128], F32)
        ONEC = sb("onec", [128, 1], F32)
        IDT = sb("idt", [128, 128], BF16)
        T11 = sb("t11", [1, 8], F32)

        (C_SS, C_SWIN, C_SX, C_SDT, C_SWOUT, C_RSIN, C_RSK, C_CFFN,
         C_RSW, C_RSX, C_RSDT, C_RSWOUT, C_EPS, C_ONE, C_NEG1,
         C_mLN2, C_LN2, C_7LN2, C_m7LN2, C_15LN2, C_8LN2, C_6LN2) = range(22)

        XT = sb("xt", [128, D], F32)
        XN = sb("xn", [128, D], F32)
        SA = sb("sa", [128, D], F32)
        SB_ = sb("sbb", [128, D], F32)
        SD = sb("sd", [128, D], F32)
        SG = sb("sg", [128, D], F32)
        GBUF = sb("gbuf", [128, D], F32)
        SRES = sb("sres", [128, D], F32)
        MSKA = sb("mska", [128, D], U8)
        XNB = sb("xnb", [128, D], BF16)
        XQB = sb("xqb", [128, D], BF16)
        XNB_FM = sb("xnbfm", [128, D], BF16)
        XQ_FM = sb("xqfm", [128, D], BF16)
        COLS = sb("colsp", [128, 16], F32)
        XCP = sb("xcp", [128, 16 * 131], F32)
        HALO = sb("halo", [128, 48], F32)
        Z = sb("zbuf", [128, DI], F32)
        XC = sb("xcbuf", [128, DI], F32)
        XCB = sb("xcbb", [128, DI], BF16)
        DLB = sb("dlb", [64, 128], BF16)
        Y = sb("ybuf", [128, DI], BF16)
        STATE = sb("state", [128, 16], F32)
        T1 = sb("tt1", [128, 128], F32)
        T2 = sb("tt2", [128, 128], F32)

        Q1 = XC[:, 0:D]
        Q2 = XC[:, D:2 * D]
        Q3 = Z[:, 0:D]
        SE = Z[:, D:2 * D]
        GROW = SA[0:1, 0:D]
        FBROWF = SB_[0:1, 0:D]
        FBROW = XNB[0:1, 0:D]
        DTBUF = XCP[:, 0:DI]

        PT = psf("pt", [128, 128], BF16)
        PT2 = psf("pt2", [128, 128], BF16)
        PA = psf("pa", [128, 128], F32)
        PB = psf("pb", [128, 128], F32)
        PF0 = psf("pf0", [128, 512], F32)
        PF1 = psf("pf1", [128, 512], F32)
        PX = psf("px", [96, 128], F32)
        PD = psf("pd", [128, 128], F32)

        sems = {e: ctx.enter_context(nc.semaphore(f"sem_{e}")) for e in ENGINES}
        block = ctx.enter_context(nc.Block())

        col = lambda tile, i: tile[:, i:i + 1]
        sq_ = S.seq

        def dma(dst, src, after=(), chain=True):
            return sq_("sync", lambda e, d=dst, s=src: e.dma_start(out=d, in_=s),
                       after=after, chain=chain)

        def V(fn, after=(), lag=0, chain=True):
            return sq_("vector", fn, after=after, lag=lag, chain=chain)

        def A(fn, after=(), lag=0, chain=True):
            return sq_("scalar", fn, after=after, lag=lag, chain=chain)

        def T(fn, after=(), chain=True, sig=False):
            return sq_("tensor", fn, after=after, chain=chain, sig=sig)

        def G(fn, after=(), lag=0, chain=True):
            return sq_("gpsimd", fn, after=after, lag=lag, chain=chain)

        def rne_s(src, t1, t2, out, msk=None):
            m = msk if msk is not None else MSKA[:]
            V(lambda e: e.tensor_scalar(t1, src, P23, P23, ALU.add, ALU.subtract))
            V(lambda e: e.tensor_scalar(t2, t1, P52, P52, ALU.add, ALU.subtract))
            V(lambda e: e.tensor_scalar(m, t2, 0.0, None, ALU.is_equal))
            return V(lambda e: e.select(out, m, t1, t2))

        def f32z_s(w32, t2, out, msk=None):
            m = msk if msk is not None else MSKA[:]
            V(lambda e: e.tensor_scalar(t2, w32, P52, P52, ALU.add, ALU.subtract))
            V(lambda e: e.tensor_scalar(m, t2, 0.0, None, ALU.is_equal))
            return V(lambda e: e.select(out, m, w32, t2))

        G(lambda e: e.memset(ONE1[:], 1.0))
        G(lambda e: e.memset(ONEC[:], 1.0))
        G(lambda e: e.memset(IDT[:], 0.0))
        G(lambda e: e.affine_select(
            out=IDT[:], in_=IDT[:], compare_op=ALU.not_equal, fill=1.0,
            base=0, pattern=[[-1, 128]], channel_multiplier=1))
        for cc, val in [(C_EPS, 1e-5), (C_ONE, 1.0), (C_NEG1, -1.0),
                        (C_mLN2, -LN2), (C_LN2, LN2), (C_7LN2, 7 * LN2),
                        (C_m7LN2, -7 * LN2), (C_15LN2, 15 * LN2),
                        (C_8LN2, 8 * LN2), (C_6LN2, 6 * LN2)]:
            G(lambda e, c=cc, v=val: e.memset(col(SC, c), v))

        def bcast11(src11, dst_col):
            T(lambda e, s=src11: e.matmul(PF0[:, 0:1], ONE1[:], s,
                                          start=True, stop=True))
            V(lambda e, d=dst_col: e.tensor_copy(d, PF0[:, 0:1]))

        def xpart_sum(col_ap, npart, dst11):
            T(lambda e, c=col_ap, n=npart: e.matmul(
                PD[0:1, 0:1], c, ONEC[0:n, :], start=True, stop=True))
            V(lambda e, d=dst11: e.tensor_copy(d, PD[0:1, 0:1]))

        dma(T11[0:1, 0:1], ss_h[:])
        dma(T11[0:1, 1:2], s_in_h[:])
        dma(T11[0:1, 2:3], s_k_h[:])
        V(lambda e: e.tensor_tensor(T11[0:1, 3:4], T11[0:1, 1:2], T11[0:1, 2:3], ALU.mult))
        V(lambda e: e.reciprocal(T11[0:1, 4:5], T11[0:1, 1:2]))
        V(lambda e: e.reciprocal(T11[0:1, 5:6], T11[0:1, 2:3]))
        bcast11(T11[0:1, 0:1], col(SC, C_SS))
        bcast11(T11[0:1, 3:4], col(SC, C_CFFN))
        bcast11(T11[0:1, 4:5], col(SC, C_RSIN))
        bcast11(T11[0:1, 5:6], col(SC, C_RSK))

        dma(GROW, gamma_h[:])
        for h in range(2):
            T(lambda e, h=h: e.matmul(PF0[:, 0:512], ONE1[:],
                                      SA[0:1, h * 512:(h + 1) * 512],
                                      start=True, stop=True))
            V(lambda e, h=h: e.tensor_copy(GAM[:, h * 512:(h + 1) * 512], PF0[:, 0:512]))
        dma(FBROW, fbias_h[:])
        V(lambda e: e.tensor_copy(FBROWF, FBROW))
        for h in range(2):
            T(lambda e, h=h: e.matmul(PF0[:, 0:512], ONE1[:],
                                      SB_[0:1, h * 512:(h + 1) * 512],
                                      start=True, stop=True))
            V(lambda e, h=h: e.tensor_copy(FBB[:, h * 512:(h + 1) * 512], PF0[:, 0:512]))

        for c in range(16):
            dma(CW[:, c * 4:(c + 1) * 4], conv_w_h[c * 128:(c + 1) * 128, :])
            dma(col(CBIA, c), conv_b_h[c * 128:(c + 1) * 128, :])
            dma(col(BDT, c), b_dt_h[c * 128:(c + 1) * 128, :])

        def quantize_weight(dram, nchunks, width, rsw_col, s_dst_col, rs_dst_col,
                            nelem, chunkbuf, sgnbuf, mskbuf, qbuf_bf16,
                            after_quant_chunk):
            for c in range(nchunks):
                dma(chunkbuf[:, 0:width], dram[c * 128:(c + 1) * 128, :])
                V(lambda e: e.reduce_sum(col(COLS, 0), chunkbuf[:, 0:width],
                                         AXX, apply_absolute_value=True))
                if c == 0:
                    V(lambda e: e.tensor_copy(col(COLS, 1), col(COLS, 0)))
                else:
                    V(lambda e: e.tensor_tensor(col(COLS, 1), col(COLS, 1),
                                                col(COLS, 0), ALU.add))
            xpart_sum(col(COLS, 1), 128, T11[0:1, 6:7])
            V(lambda e, n=nelem: e.tensor_scalar(T11[0:1, 6:7], T11[0:1, 6:7],
                                                 1.0 / n, 1e-8, ALU.mult, ALU.add))
            V(lambda e: e.reciprocal(T11[0:1, 7:8], T11[0:1, 6:7]))
            bcast11(T11[0:1, 6:7], s_dst_col)
            bcast11(T11[0:1, 7:8], rs_dst_col)
            for c in range(nchunks):
                dma(chunkbuf[:, 0:width], dram[c * 128:(c + 1) * 128, :])
                A(lambda e: e.activation(sgnbuf[:, 0:width], chunkbuf[:, 0:width], AF.Sign))
                V(lambda e, rc=rsw_col: e.tensor_scalar(
                    mskbuf[:, 0:width], chunkbuf[:, 0:width], rc, None, ALU.mult))
                A(lambda e: e.activation(mskbuf[:, 0:width], mskbuf[:, 0:width], AF.Abs))
                V(lambda e: e.tensor_scalar(mskbuf[:, 0:width], mskbuf[:, 0:width],
                                            0.5, None, ALU.is_gt))
                V(lambda e: e.tensor_tensor(qbuf_bf16[:, 0:width], sgnbuf[:, 0:width],
                                            mskbuf[:, 0:width], ALU.mult))
                after_quant_chunk(c)

        def win_chunk(c):
            for k in range(NKD):
                T(lambda e, k=k: e.transpose(PT[:], XNB[:, k * 128:(k + 1) * 128], IDT[:]))
                V(lambda e, k=k, c=c: e.tensor_copy(
                    WIN[k][:, c * 128:(c + 1) * 128], PT[:]))
        quantize_weight(w_in_h, 32, D, col(SC, C_RSW), col(SC, C_SWIN),
                        col(SC, C_RSW), 4096 * 1024, XT, SA, SB_, XNB, win_chunk)

        def wout_chunk(c):
            for k in range(NCH):
                T(lambda e, k=k: e.transpose(PT[:], XCB[:, k * 128:(k + 1) * 128], IDT[:]))
                V(lambda e, k=k, c=c: e.tensor_copy(
                    WOUT[k][:, c * 128:(c + 1) * 128], PT[:]))
        quantize_weight(w_out_h, 8, DI, col(SC, C_RSWOUT), col(SC, C_SWOUT),
                        col(SC, C_RSWOUT), 1024 * 2048, Z, XCP[:, 0:DI],
                        XC[:, 0:DI], XCB, wout_chunk)

        def wdt_chunk(c):
            T(lambda e: e.transpose(PT[0:64, :], XNB[:, 0:64], IDT[:]))
            V(lambda e, c=c: e.tensor_copy(WDT[:, c * 128:(c + 1) * 128], PT[0:64, :]))
        quantize_weight(w_dt_h, 16, 64, col(SC, C_RSDT), col(SC, C_SDT),
                        col(SC, C_RSDT), 2048 * 64, XT, SA, SB_, XNB, wdt_chunk)

        dma(XCP[0:96, 0:DI], w_x_h[:])
        V(lambda e: e.reduce_sum(COLS[0:96, 0:1], XCP[0:96, 0:DI], AXX,
                                 apply_absolute_value=True))
        xpart_sum(COLS[0:96, 0:1], 96, T11[0:1, 6:7])
        V(lambda e: e.tensor_scalar(T11[0:1, 6:7], T11[0:1, 6:7],
                                    1.0 / (96 * 2048), 1e-8, ALU.mult, ALU.add))
        V(lambda e: e.reciprocal(T11[0:1, 7:8], T11[0:1, 6:7]))
        bcast11(T11[0:1, 6:7], col(SC, C_SX))
        bcast11(T11[0:1, 7:8], col(SC, C_RSX))
        A(lambda e: e.activation(Z[0:96, 0:DI], XCP[0:96, 0:DI], AF.Sign))
        V(lambda e: e.tensor_scalar(XC[0:96, 0:DI], XCP[0:96, 0:DI],
                                    SC[0:96, C_RSX:C_RSX + 1], None, ALU.mult))
        A(lambda e: e.activation(XC[0:96, 0:DI], XC[0:96, 0:DI], AF.Abs))
        V(lambda e: e.tensor_scalar(XC[0:96, 0:DI], XC[0:96, 0:DI], 0.5, None, ALU.is_gt))
        V(lambda e: e.tensor_tensor(XCB[0:96, 0:DI], Z[0:96, 0:DI],
                                    XC[0:96, 0:DI], ALU.mult))
        for c in range(16):
            T(lambda e, c=c: e.transpose(PT[:, 0:96], XCB[0:96, c * 128:(c + 1) * 128],
                                         IDT[0:96, 0:96]))
            V(lambda e, c=c: e.tensor_copy(WXT[:, c * 96:(c + 1) * 96], PT[:, 0:96]))

        def quant_fp8(vbuf, outbuf_bf16):
            A(lambda e: e.activation(SD[:], vbuf, AF.Abs))
            V(lambda e: e.tensor_scalar_max(SD[:], SD[:], 1e-20))
            A(lambda e: e.activation(SE, SD[:], AF.Ln))
            V(lambda e: e.tensor_scalar(SE, SE, RLN2, 255.5, ALU.mult, ALU.add))
            rne_s(SE, SD[:], SG[:], SE)
            V(lambda e: e.tensor_scalar(SE, SE, -256.0, -6.0, ALU.add, ALU.max))
            V(lambda e: e.tensor_scalar_min(SE, SE, 8.0))
            A(lambda e: e.activation(SD[:], SE, AF.Exp,
                                     bias=col(SC, C_8LN2), scale=col(SC, C_mLN2)))
            rne_s(SD[:], SG[:], SRES[:], SD[:])
            V(lambda e: e.tensor_tensor(SG[:], vbuf, SD[:], ALU.mult))
            V(lambda e: e.tensor_scalar(SG[:], SG[:], float(2.0 ** 15), None, ALU.mult))
            f32z_s(SG[:], SRES[:], GBUF[:])
            V(lambda e: e.tensor_scalar(GBUF[:], GBUF[:], float(2.0 ** -20), 1024.0,
                                        ALU.mult, ALU.add))
            rne_s(GBUF[:], SG[:], SRES[:], GBUF[:])
            A(lambda e: e.activation(SD[:], SE, AF.Exp,
                                     bias=col(SC, C_6LN2), scale=col(SC, C_LN2)))
            rne_s(SD[:], SG[:], SRES[:], SD[:])
            V(lambda e: e.tensor_scalar(SD[:], SD[:], float(2.0 ** -9), None, ALU.mult))
            V(lambda e: e.scalar_tensor_tensor(SRES[:], GBUF[:], -1024.0, SD[:],
                                               ALU.add, ALU.mult))
            V(lambda e: e.tensor_copy(outbuf_bf16, SRES[:]))

        for k in range(NKD):
            dma(XQB[:], kern_h[k * 128:(k + 1) * 128, :])
            V(lambda e: e.tensor_copy(SA[:], XQB[:]))
            V(lambda e: e.tensor_scalar(SA[:], SA[:], col(SC, C_RSK), None, ALU.mult))
            quant_fp8(SA[:], XQB[:])
            V(lambda e, k=k: e.tensor_copy(KQ[k][:], XQB[:]))

        def soft_bf16_x2(src, dst):
            A(lambda e: e.activation(SA[:], src, AF.Abs))
            V(lambda e: e.tensor_scalar_max(SA[:], SA[:], 1e-20))
            A(lambda e: e.activation(SB_[:], SA[:], AF.Ln))
            V(lambda e: e.tensor_scalar(SB_[:], SB_[:], RLN2, 255.5, ALU.mult, ALU.add))
            rne_s(SB_[:], SD[:], SA[:], SE)
            V(lambda e: e.tensor_scalar(SE, SE, -256.0, -15.0, ALU.add, ALU.max))
            V(lambda e: e.tensor_scalar_min(SE, SE, 6.0))
            A(lambda e: e.activation(SA[:], SE, AF.Exp,
                                     bias=col(SC, C_mLN2), scale=col(SC, C_mLN2)))
            rne_s(SA[:], SB_[:], SD[:], SA[:])
            A(lambda e: e.activation(SB_[:], SE, AF.Exp,
                                     bias=col(SC, C_7LN2), scale=col(SC, C_mLN2)))
            rne_s(SB_[:], SD[:], Q1, SB_[:])
            V(lambda e: e.tensor_scalar(MSKA[:], SE, -7.5, None, ALU.is_ge))
            V(lambda e: e.tensor_scalar(SA[:], SA[:], 256.0, None, ALU.mult))
            V(lambda e: e.select(SD[:], MSKA[:], SB_[:], SA[:]))
            V(lambda e: e.tensor_tensor(Q1, src, SD[:], ALU.mult))
            V(lambda e: e.tensor_scalar(Q1, Q1, 65536.0, None, ALU.mult))
            f32z_s(Q1, Q2, Q3)
            V(lambda e: e.tensor_scalar(Q3, Q3, float(2.0 ** -16), 1024.0,
                                        ALU.mult, ALU.add))
            rne_s(Q3, Q1, Q2, Q3)
            A(lambda e: e.activation(SA[:], SE, AF.Exp,
                                     bias=col(SC, C_15LN2), scale=col(SC, C_LN2)))
            rne_s(SA[:], SB_[:], SD[:], SA[:])
            V(lambda e: e.tensor_scalar(SA[:], SA[:], float(2.0 ** -22), None, ALU.mult))
            A(lambda e: e.activation(SB_[:], SE, AF.Exp,
                                     bias=col(SC, C_7LN2), scale=col(SC, C_LN2)))
            rne_s(SB_[:], SD[:], Q1, SB_[:])
            V(lambda e: e.tensor_scalar(SB_[:], SB_[:], float(2.0 ** -14), None, ALU.mult))
            V(lambda e: e.tensor_scalar(MSKA[:], SE, -1.5, None, ALU.is_ge))
            V(lambda e: e.select(SD[:], MSKA[:], SB_[:], SA[:]))
            V(lambda e: e.scalar_tensor_tensor(dst, Q3, -1024.0, SD[:],
                                               ALU.add, ALU.mult))

        def soft_bf16_wide(src, dst, ta, tb, tc, td):
            A(lambda e: e.activation(ta, src, AF.Abs))
            V(lambda e: e.tensor_scalar_max(ta, ta, 1e-30))
            A(lambda e: e.activation(tb, ta, AF.Ln))
            V(lambda e: e.tensor_scalar(tb, tb, RLN2, 255.5, ALU.mult, ALU.add))
            rne_s(tb, ta, tc, tb)
            V(lambda e: e.tensor_scalar(tb, tb, -256.0, -30.0, ALU.add, ALU.max))
            V(lambda e: e.tensor_scalar_min(tb, tb, 12.0))
            A(lambda e: e.activation(ta, tb, AF.Exp,
                                     bias=col(SC, C_7LN2), scale=col(SC, C_mLN2)))
            V(lambda e: e.tensor_tensor(tc, src, ta, ALU.mult))
            V(lambda e: e.tensor_scalar(tc, tc, 65536.0, None, ALU.mult))
            f32z_s(tc, ta, td)
            V(lambda e: e.tensor_scalar(td, td, float(2.0 ** -16), 1024.0,
                                        ALU.mult, ALU.add))
            rne_s(td, ta, tc, td)
            A(lambda e: e.activation(ta, tb, AF.Exp,
                                     bias=col(SC, C_m7LN2), scale=col(SC, C_LN2)))
            V(lambda e: e.scalar_tensor_tensor(dst, td, -1024.0, ta,
                                               ALU.add, ALU.mult))

        def sp_half(xin, dst, ta, tb, tc):
            A(lambda e: e.activation(ta, xin, AF.Square))
            V(lambda e: e.tensor_scalar(ta, ta, 4.0, 1e6, ALU.add, ALU.min))
            V(lambda e: e.tensor_scalar(tb, ta, 16.0, -0.375, ALU.is_gt, ALU.mult))
            V(lambda e: e.tensor_scalar(tc, ta, 64.0, -0.0625, ALU.is_gt, ALU.mult))
            V(lambda e: e.scalar_tensor_tensor(tb, tb, 0.5, tc, ALU.add, ALU.add))
            V(lambda e: e.tensor_scalar(tc, ta, 256.0, -0.03125, ALU.is_gt, ALU.mult))
            V(lambda e: e.tensor_tensor(tb, tb, tc, ALU.add))
            for it in range(3):
                V(lambda e: e.tensor_tensor(tc, tb, tb, ALU.mult))
                V(lambda e: e.scalar_tensor_tensor(tc, tc, -0.5, ta, ALU.mult, ALU.mult))
                V(lambda e: e.scalar_tensor_tensor(tb, tc, 1.5, tb, ALU.add, ALU.mult))
                if it == 0:
                    V(lambda e: e.tensor_scalar_max(tb, tb, 1e-6))
            V(lambda e: e.scalar_tensor_tensor(tc, ta, 0.5, tb, ALU.mult, ALU.mult))
            V(lambda e: e.scalar_tensor_tensor(dst, xin, 0.5, tc, ALU.mult, ALU.add))

        vec_now = S.now("vector")
        ev_tok = [vec_now, vec_now]

        xcp3 = XCP[:].rearrange("p (c w) -> p c w", w=131)
        halo3 = HALO[:].rearrange("p (c w) -> p c w", w=3)

        for tile in range(n_tiles):
            seq_start = (tile % seq_tiles) == 0
            r0 = tile * TOK

            dma(XT[:], x_h[r0:r0 + TOK, :])

            A(lambda e: e.activation(SA[:], XT[:], AF.Square, accum_out=col(COLS, 0)))
            A(lambda e: e.activation(SA[:], XT[:], AF.Identity, accum_out=col(COLS, 1)))
            V(lambda e: e.tensor_scalar(col(COLS, 2), col(COLS, 1), 1.0 / D, None, ALU.mult))
            V(lambda e: e.tensor_scalar(col(COLS, 3), col(COLS, 0), 1.0 / D, None, ALU.mult))
            V(lambda e: e.tensor_tensor(col(COLS, 4), col(COLS, 2), col(COLS, 2), ALU.mult))
            V(lambda e: e.tensor_tensor(col(COLS, 5), col(COLS, 3), col(COLS, 4), ALU.subtract))
            A(lambda e: e.activation(col(COLS, 6), col(COLS, 5), AF.Sqrt,
                                     bias=col(SC, C_EPS)))
            V(lambda e: e.reciprocal(col(COLS, 7), col(COLS, 6)))
            V(lambda e: e.tensor_scalar(XN[:], XT[:], col(COLS, 2), col(COLS, 7),
                                        ALU.subtract, ALU.mult))

            A(lambda e: e.activation(SA[:], XN[:], AF.Identity, accum_out=col(COLS, 8)))
            V(lambda e: e.tensor_scalar(col(COLS, 9), col(COLS, 8), 1.0 / D, None, ALU.mult))
            V(lambda e: e.tensor_scalar(SA[:], XN[:], col(COLS, 9), None, ALU.subtract))
            A(lambda e: e.activation(SB_[:], SA[:], AF.Square, accum_out=col(COLS, 10)))
            V(lambda e: e.tensor_scalar(col(COLS, 11), col(COLS, 10), 1.0 / D, 1e-9,
                                        ALU.mult, ALU.add))
            for i, (thr, dlt) in enumerate(
                    [(0.0625, -2.0), (0.25, -1.0), (4.0, -0.5), (16.0, -0.25),
                     (64.0, -0.125), (256.0, -0.0625), (1024.0, -0.03125),
                     (4096.0, -0.015625), (16384.0, -0.0078125),
                     (65536.0, -0.00390625)]):
                V(lambda e, i=i, t_=thr, d_=dlt: e.tensor_scalar(
                    col(COLS, 4 + i), col(COLS, 11), t_, d_, ALU.is_ge, ALU.mult),
                  lag=i)
            V(lambda e: e.reduce_sum(col(COLS, 14), COLS[:, 4:14], AXX))
            V(lambda e: e.tensor_scalar(col(COLS, 14), col(COLS, 14), 4.0, None, ALU.add))
            V(lambda e: e.tensor_tensor(col(COLS, 14), col(COLS, 14),
                                        col(SC, C_SS), ALU.mult))
            V(lambda e: e.tensor_scalar(SA[:], SA[:], col(COLS, 14), None, ALU.mult))
            V(lambda e: e.tensor_tensor(XNB[:], SA[:], GAM[:], ALU.mult))

            soft_bf16_x2(XN[:], SG[:])
            if debug and tile == 0:
                dma(dbg["xn"], XN[:])
                dma(dbg["x2"], SG[:])
            V(lambda e: e.tensor_scalar(SA[:], SG[:], col(SC, C_RSIN), None, ALU.mult))
            quant_fp8(SA[:], XQB[:])
            if debug and tile == 0:
                dma(dbg["xqb"], XQB[:])

            vcur = S.now("vector")
            for idx in range(16):
                srcb, dstb = (XNB, XNB_FM) if idx < 8 else (XQB, XQ_FM)
                k = idx % 8
                pt = PT if idx % 2 == 0 else PT2
                ttok = T(lambda e, k=k, s=srcb, p=pt: e.transpose(
                    p[:], s[:, k * 128:(k + 1) * 128], IDT[:]),
                    after=[ev_tok[idx % 2], vcur], chain=False, sig=True)
                ev_tok[idx % 2] = V(lambda e, k=k, d=dstb, p=pt: e.tensor_copy(
                    d[:, k * 128:(k + 1) * 128], p[:]),
                    after=[ttok], chain=False)
            S.last_eng = "vector"

            ffn_tok = []
            for h, PF in ((0, PF0), (1, PF1)):
                tk = None
                for k in range(NKD):
                    tk = T(lambda e, k=k, h=h, PF=PF: e.matmul(
                        PF[:], XQ_FM[:, k * 128:(k + 1) * 128],
                        KQ[k][:, h * 512:(h + 1) * 512],
                        start=(k == 0), stop=(k == NKD - 1)),
                        after=([ev_tok[0], ev_tok[1]] if k == 0 else []),
                        chain=False, sig=(k == NKD - 1))
                ffn_tok.append(tk)
            for h, PF in ((0, PF0), (1, PF1)):
                V(lambda e, h=h, PF=PF: e.tensor_scalar(
                    SA[:, h * 512:(h + 1) * 512], PF[:], col(SC, C_CFFN), None, ALU.mult),
                    after=[ffn_tok[h]], chain=False)
            S.last_eng = "vector"
            soft_bf16_wide(SA[:], SD[:], SB_[:], SE, GBUF[:], SRES[:])
            V(lambda e: e.tensor_tensor(GBUF[:], SD[:], FBB[:], ALU.add))
            sil_tok = A(lambda e: e.activation(SG[:], GBUF[:], AF.Silu))
            if debug and tile == 0:
                dma(dbg["g"], SG[:])

            if seq_start:
                V(lambda e: e.memset(xcp3[:, :, 0:3], 0.0))
                hal_tok = V(lambda e: e.memset(STATE[:], 0.0))
            else:
                hal_tok = V(lambda e: e.tensor_copy(xcp3[:, :, 0:3], halo3[:, :, :]))
            ev_win = [None, None]
            for m in range(32):
                PM = PA if (m % 2 == 0) else PB
                aft = []
                if m == 0:
                    aft = [hal_tok, sil_tok]
                elif ev_win[m % 2] is not None:
                    aft = [ev_win[m % 2]]
                tk = None
                for k in range(NKD):
                    tk = T(lambda e, m=m, k=k, PM=PM: e.matmul(
                        PM[:], WIN[k][:, m * 128:(m + 1) * 128],
                        XNB_FM[:, k * 128:(k + 1) * 128],
                        start=(k == 0), stop=(k == NKD - 1)),
                        after=(aft if k == 0 else []), chain=False,
                        sig=(k == NKD - 1))
                if m < 16:
                    ev_win[m % 2] = V(lambda e, m=m, PM=PM: e.tensor_scalar(
                        XCP[:, m * 131 + 3:m * 131 + 131], PM[:],
                        col(SC, C_SWIN), None, ALU.mult),
                        after=[tk], chain=False)
                else:
                    ev_win[m % 2] = V(lambda e, m=m, PM=PM: e.tensor_scalar(
                        Z[:, (m - 16) * 128:(m - 15) * 128], PM[:],
                        col(SC, C_SWIN), None, ALU.mult),
                        after=[tk], chain=False)
            S.last_eng = "vector"
            V(lambda e: e.tensor_copy(halo3[:, :, :], xcp3[:, :, 128:131]))
            if debug and tile == 0:
                dma(dbg["xcp"], XCP[:])

            for h in range(2):
                for j in range(8):
                    m = h * 8 + j
                    base = m * 131
                    V(lambda e, b=base, m=m: e.tensor_scalar(
                        T1[:], XCP[:, b:b + 128], col(CW, 4 * m), None, ALU.mult))
                    V(lambda e, b=base, m=m: e.scalar_tensor_tensor(
                        T2[:], XCP[:, b + 1:b + 129], col(CW, 4 * m + 1), T1[:],
                        ALU.mult, ALU.add))
                    V(lambda e, b=base, m=m: e.tensor_scalar(
                        T1[:], XCP[:, b + 3:b + 131], col(CW, 4 * m + 3), col(CBIA, m),
                        ALU.mult, ALU.add))
                    V(lambda e, b=base, m=m: e.scalar_tensor_tensor(
                        T1[:], XCP[:, b + 2:b + 130], col(CW, 4 * m + 2), T1[:],
                        ALU.mult, ALU.add))
                    V(lambda e, j=j: e.tensor_tensor(
                        SA[:, j * 128:(j + 1) * 128], T2[:], T1[:], ALU.add))
                if debug and tile == 0 and h == 0:
                    dma(dbg["convo"], SA[:])
                sp_half(SA[:], XC[:, h * D:(h + 1) * D], SB_[:], SD[:], GBUF[:])
            for h in range(2):
                zh = Z[:, h * D:(h + 1) * D]
                sp_half(zh, zh, SB_[:], SD[:], GBUF[:])
            xcb_tok = V(lambda e: e.tensor_copy(XCB[:], XC[:]))
            if debug and tile == 0:
                dma(dbg["xc"], XC[:])
                dma(dbg["z"], Z[:])

            wx_tok = None
            for c in range(16):
                wx_tok = T(lambda e, c=c: e.matmul(
                    PX[:], WXT[:, c * 96:(c + 1) * 96], XCB[:, c * 128:(c + 1) * 128],
                    start=(c == 0), stop=(c == 15)),
                    after=([xcb_tok] if c == 0 else []), chain=False, sig=(c == 15))
            dlb_tok = V(lambda e: e.tensor_scalar(
                DLB[:], PX[0:64, :], SC[0:64, C_SX:C_SX + 1], None, ALU.mult),
                after=[wx_tok], chain=False)
            S.last_eng = "vector"

            ev_dt = [dlb_tok, dlb_tok]
            for m in range(16):
                PM = PD if (m % 2 == 0) else PA
                tk = T(lambda e, m=m, PM=PM: e.matmul(
                    PM[:], WDT[:, m * 128:(m + 1) * 128], DLB[:],
                    start=True, stop=True), after=[ev_dt[m % 2]],
                    chain=False, sig=True)
                ev_dt[m % 2] = A(lambda e, m=m, PM=PM: e.activation(
                    DTBUF[:, m * 128:(m + 1) * 128], PM[:], AF.Identity,
                    bias=col(BDT, m), scale=col(SC, C_SDT)),
                    after=[tk], chain=False)
            S.last_eng = "scalar"

            ylast = None
            for h in range(2):
                dth = DTBUF[:, h * D:(h + 1) * D]
                zh = Z[:, h * D:(h + 1) * D]
                A(lambda e, dth=dth: e.activation(SA[:], dth, AF.Abs))
                A(lambda e: e.activation(SB_[:], SA[:], AF.Ln, bias=col(SC, C_ONE)))
                A(lambda e: e.activation(SD[:], SB_[:], AF.Exp, scale=col(SC, C_NEG1)))
                V(lambda e: e.scalar_tensor_tensor(SB_[:], SA[:], 1.0, SD[:],
                                                   ALU.add, ALU.mult))
                V(lambda e: e.tensor_scalar(SB_[:], SB_[:], -1.0, 2.0, ALU.mult, ALU.add))
                V(lambda e: e.tensor_tensor(SD[:], SD[:], SB_[:], ALU.mult))
                V(lambda e, dth=dth: e.tensor_tensor(SA[:], dth, SD[:], ALU.mult))
                V(lambda e: e.tensor_scalar(SA[:], SA[:], 0.5, 0.5, ALU.mult, ALU.add))
                V(lambda e: e.tensor_scalar(SA[:], SA[:], 32768.0, 2.0, ALU.mult, ALU.add))
                rne_s(SA[:], SD[:], GBUF[:], SA[:])
                V(lambda e: e.tensor_scalar(SB_[:], SA[:], float(2.0 ** -15),
                                            float(-2.0 ** -14), ALU.mult, ALU.add))
                V(lambda e: e.tensor_scalar(GBUF[:], SA[:], float(-2.0 ** -15),
                                            1.00006103515625, ALU.mult, ALU.add))
                V(lambda e, h=h: e.tensor_tensor(SD[:], GBUF[:],
                                                 XC[:, h * D:(h + 1) * D], ALU.mult))
                for j in range(8):
                    m = h * 8 + j
                    V(lambda e, j=j, m=m: e.tensor_tensor_scan(
                        GBUF[:, j * 128:(j + 1) * 128],
                        SB_[:, j * 128:(j + 1) * 128],
                        SD[:, j * 128:(j + 1) * 128],
                        col(STATE, m), ALU.mult, ALU.add), lag=j)
                for j in range(8):
                    m = h * 8 + j
                    V(lambda e, j=j, m=m: e.tensor_copy(
                        col(STATE, m), GBUF[:, j * 128 + 127:j * 128 + 128]), lag=7)
                for j in range(8):
                    m = h * 8 + j
                    ylast = V(lambda e, j=j, m=m, zh=zh: e.tensor_tensor(
                        Y[:, m * 128:(m + 1) * 128], GBUF[:, j * 128:(j + 1) * 128],
                        zh[:, j * 128:(j + 1) * 128], ALU.mult), lag=7)
            if debug and tile == 0:
                dma(dbg["y"], Y[:])

            wo_tok = []
            for h, PF in ((0, PF0), (1, PF1)):
                tk = None
                for k in range(NCH):
                    tk = T(lambda e, k=k, h=h, PF=PF: e.matmul(
                        PF[:], Y[:, k * 128:(k + 1) * 128],
                        WOUT[k][:, h * 512:(h + 1) * 512],
                        start=(k == 0), stop=(k == NCH - 1)),
                        after=([ylast] if k == 0 else []),
                        chain=False, sig=(k == NCH - 1))
                wo_tok.append(tk)
            for h, PF in ((0, PF0), (1, PF1)):
                V(lambda e, h=h, PF=PF: e.scalar_tensor_tensor(
                    SRES[:, h * 512:(h + 1) * 512], PF[:], col(SC, C_SWOUT),
                    XN[:, h * 512:(h + 1) * 512], ALU.mult, ALU.add),
                    after=[wo_tok[h]], chain=False)
            S.last_eng = "vector"
            if debug and tile == 0:
                dma(dbg["sres"], SRES[:])

            V(lambda e: e.tensor_tensor(SA[:], SRES[:], SG[:], ALU.mult))
            V(lambda e: e.tensor_tensor(SA[:], SA[:], XT[:], ALU.add))
            dma(out_h[r0:r0 + TOK, :], SA[:])

        G(lambda e: e.engine_nop())

        @block.sync
        def _(sync):
            S.emit(sync, "sync", sems)

        @block.tensor
        def _(tensor):
            S.emit(tensor, "tensor", sems)

        @block.vector
        def _(vector):
            S.emit(vector, "vector", sems)

        @block.scalar
        def _(scalar):
            S.emit(scalar, "scalar", sems)

        @block.gpsimd
        def _(gpsimd):
            S.emit(gpsimd, "gpsimd", sems)

    return nc


_NC_CACHE = {}


def _get_nc(n_tiles, seq_tiles):
    key = (n_tiles, seq_tiles)
    if key not in _NC_CACHE:
        _NC_CACHE[key] = build_nc(n_tiles, seq_tiles)
    return _NC_CACHE[key]


def _in_map(x_shard, inputs):
    f32 = lambda a: np.ascontiguousarray(np.asarray(a), dtype=np.float32)
    return {
        "x": f32(x_shard).reshape(-1, D),
        "gamma": f32(inputs["L0_attn_gamma"]).reshape(1, D),
        "step_size": f32(inputs["L0_attn_step_size"]).reshape(1, 1),
        "w_in": f32(inputs["L0_attn_W_in"]),
        "conv_w": f32(inputs["L0_attn_conv_w"]).reshape(DI, 4),
        "conv_b": f32(inputs["L0_attn_conv_b"]).reshape(DI, 1),
        "w_x": f32(inputs["L0_attn_W_x"]),
        "w_dt": f32(inputs["L0_attn_W_dt"]),
        "b_dt": f32(inputs["L0_attn_b_dt"]).reshape(DI, 1),
        "w_out": f32(inputs["L0_attn_W_out"]),
        "ffn_kernel": np.ascontiguousarray(
            np.asarray(inputs["L0_ffn_kernel"]), dtype=ml_dtypes.bfloat16),
        "ffn_input_scale": f32(inputs["L0_ffn_input_scale"]).reshape(1, 1),
        "ffn_kernel_scale": f32(inputs["L0_ffn_kernel_scale"]).reshape(1, 1),
        "ffn_bias": np.ascontiguousarray(
            np.asarray(inputs["L0_ffn_bias"]), dtype=ml_dtypes.bfloat16).reshape(1, D),
    }


def kernel(**inputs):
    x = np.asarray(inputs["L0_attn_x"])
    B, L, _ = x.shape
    per = B // N_CORES
    nc = _get_nc((per * L) // TOK, L // TOK)
    in_maps = [_in_map(x[i * per:(i + 1) * per], inputs) for i in range(N_CORES)]
    res = run_bass_kernel_spmd(nc, in_maps, core_ids=list(range(N_CORES)))
    outs = [np.asarray(r["out"], dtype=np.float32).reshape(per, L, D)
            for r in res.results]
    return np.concatenate(outs, axis=0)


# version 04
# speedup vs baseline: 1.8246x; 1.0624x over previous
import numpy as np
import ml_dtypes
import concourse.bass as bass
import concourse.mybir as mybir
from concourse.bass_utils import run_bass_kernel_spmd
from contextlib import ExitStack

F32 = mybir.dt.float32
BF16 = mybir.dt.bfloat16
U8 = mybir.dt.uint8
ALU = mybir.AluOpType
AF = mybir.ActivationFunctionType
AXX = mybir.AxisListType.X

LN2 = float(np.log(2.0))
RLN2 = float(1.0 / np.log(2.0))
P23 = float(2.0 ** 23)
P52 = float(2.0 ** 52)

D = 1024
DI = 2048
TOK = 128
NKD = D // 128
NCH = DI // 128
N_CORES = 8

ENGINES = ("sync", "tensor", "vector", "scalar", "gpsimd")
VSG = ("vector", "scalar", "gpsimd")


def _ref_rne(i0, i1, s0, s1, imm2):
    t1 = (i0 + s0) - s0
    t2 = (t1 + s1) - s1
    return np.where(t2 == 0, t1, t2)


def _ref_f32z(i0, i1, s0, s1, imm2):
    w = i0 * imm2
    t2 = (w + s1) - s1
    return np.where(t2 == 0, w, t2)


def _ref_sqy(i0, i1, s0, s1, imm2):
    return np.minimum(i0 * i0 + 4.0, s0)


def _ref_sqr0a(i0, i1, s0, s1, imm2):
    return np.where(i0 > s0 * 4.0, s1 * imm2,
                    np.where(i0 > s0, s1, imm2)).astype(i0.dtype)


def _ref_sqr0b(i0, i1, s0, s1, imm2):
    return np.where(i0 > s0, np.asarray(s1, dtype=i0.dtype), i1)


def _ref_sqit1(i0, i1, s0, s1, imm2):
    return np.maximum(((i0 * i0) * i1 * s0 + s1) * i0, imm2)


def _ref_sqit(i0, i1, s0, s1, imm2):
    return ((i0 * i0) * i1 * s0 + s1) * i0


def _ref_dtq(i0, i1, s0, s1, imm2):
    av = np.maximum(i0, 0.0 - i0)
    t = (av + 1.0) * i1
    t = 2.0 - t
    t = i1 * t
    return i0 * t


def _ref_dtb(i0, i1, s0, s1, imm2):
    return (i0 * s0 + s1) * i1


_OPS = {}


def _register_custom_ops():
    if _OPS:
        return _OPS
    import concourse.dve_ops as dom
    from concourse.dve_op import DveOp
    from concourse.dve_spec import (Spec, Src0, Src1, C0, C1, C2, Zero, One,
                                    select, eq, sq, maxx, minn, lower)
    from concourse.dve_table_gen import dve_ver_for
    ver = dve_ver_for("TRN2")
    K2 = One + One
    K4 = K2 * K2

    def reg(name, body, ref, row, perf=False):
        existing = next((o for o in dom.OPS if o.name == name), None)
        if existing is not None:
            _OPS[name] = existing
            return
        spec = Spec(body=body)
        try:
            low = lower(spec, ver=ver, perf_en=perf, name=name)
        except Exception:
            perf = False
            low = lower(spec, ver=ver, perf_en=False, name=name)
        op = DveOp(name, spec, subdim=False, uops_sha={ver: low.sha(ver)},
                   perf_en={ver: perf} if perf else {}, reference=ref)
        dom.OPS.append(op)
        dom.CUSTOM_DVE_SPECS[name] = spec
        dom.CUSTOM_DVE_REFS[name] = ref
        dom._SUB_OPCODE_FOR_NAME[name] = row
        _OPS[name] = op

    t1 = (Src0 + C0) - C0
    t2 = (t1 + C1) - C1
    reg("ANT_RNE", select(eq(t2, Zero), t1, t2), _ref_rne, 1)
    w = Src0 * C2
    tw = (w + C1) - C1
    reg("ANT_F32Z", select(eq(tw, Zero), w, tw), _ref_f32z, 2)
    reg("ANT_SQY", minn(sq(Src0) + K4, C0), _ref_sqy, 3)
    reg("ANT_SQR0A",
        select(Src0 > C0 * K4, C1 * C2, select(Src0 > C0, C1, C2)),
        _ref_sqr0a, 4)
    reg("ANT_SQR0B", select(Src0 > C0, C1, Src1), _ref_sqr0b, 5)
    reg("ANT_SQIT1", maxx(((sq(Src0) * Src1) * C0 + C1) * Src0, C2),
        _ref_sqit1, 6)
    reg("ANT_SQIT", ((sq(Src0) * Src1) * C0 + C1) * Src0, _ref_sqit, 7)
    av = maxx(Src0, Zero - Src0)
    reg("ANT_DTQ", Src0 * (Src1 * (K2 - (av + One) * Src1)), _ref_dtq, 8)
    reg("ANT_DTB", (Src0 * C0 + C1) * Src1, _ref_dtb, 9)
    return _OPS


class Sched:

    def __init__(self):
        self.plan = {e: [] for e in ENGINES}
        self.counts = {e: 0 for e in ENGINES}
        self.last_eng = None

    def _ensure_sig(self, eng):
        fn0, deps0, sig0 = self.plan[eng][-1]
        if sig0 == 0:
            amt = 16 if eng == "sync" else 1
            self.plan[eng][-1] = (fn0, deps0, amt)
            self.counts[eng] += amt
        return (eng, self.counts[eng])

    def seq(self, eng, fn, after=(), lag=0, chain=True, sig=False):
        deps = {}
        for t in after:
            if t is None:
                continue
            e2, vv = t
            if e2 != eng or eng not in VSG:
                deps[e2] = max(deps.get(e2, 0), vv)
        if chain and self.last_eng is not None and self.last_eng != eng:
            le = self.last_eng
            tok = self._ensure_sig(le)
            deps[le] = max(deps.get(le, 0), tok[1])
        if eng in VSG:
            own = self.counts[eng] - lag
            for t in after:
                if t is not None and t[0] == eng:
                    own = max(own, t[1])
            if own > 0:
                deps[eng] = max(deps.get(eng, 0), own)
            self.plan[eng].append((fn, tuple(deps.items()), 1))
            self.counts[eng] += 1
        elif eng == "sync":
            self.plan[eng].append((fn, tuple(deps.items()), 16))
            self.counts["sync"] += 16
        else:
            amt = 1 if sig else 0
            self.plan[eng].append((fn, tuple(deps.items()), amt))
            self.counts[eng] += amt
        if chain:
            self.last_eng = eng
        return (eng, self.counts[eng])

    def now(self, eng):
        return (eng, self.counts[eng])

    def emit(self, engine_obj, eng_name, sems):
        for fn, deps, sig in self.plan[eng_name]:
            for (e2, vv) in deps:
                engine_obj.wait_ge(sems[e2], vv)
            inst = fn(engine_obj)
            if sig:
                inst.then_inc(sems[eng_name], sig)


def build_nc(n_tiles=64, seq_tiles=16, debug=False):
    nc = bass.Bass("TRN2", target_bir_lowering=False)
    n_rows = n_tiles * TOK

    def din(name, shape, dt_):
        return nc.dram_tensor(name, shape, dt_, kind="ExternalInput").ap()

    x_h = din("x", [n_rows, D], F32)
    gamma_h = din("gamma", [1, D], F32)
    ss_h = din("step_size", [1, 1], F32)
    w_in_h = din("w_in", [2 * DI, D], F32)
    conv_w_h = din("conv_w", [DI, 4], F32)
    conv_b_h = din("conv_b", [DI, 1], F32)
    w_x_h = din("w_x", [96, DI], F32)
    w_dt_h = din("w_dt", [DI, 64], F32)
    b_dt_h = din("b_dt", [DI, 1], F32)
    w_out_h = din("w_out", [D, DI], F32)
    kern_h = din("ffn_kernel", [D, D], BF16)
    s_in_h = din("ffn_input_scale", [1, 1], F32)
    s_k_h = din("ffn_kernel_scale", [1, 1], F32)
    fbias_h = din("ffn_bias", [1, D], BF16)
    out_h = nc.dram_tensor("out", [n_rows, D], F32, kind="ExternalOutput").ap()
    dbg = {}
    if debug:
        for nm, w_, dt_ in [("xn", D, F32), ("x2", D, F32), ("xqb", D, BF16),
                            ("g", D, F32), ("z", DI, F32), ("xc", DI, F32),
                            ("y", DI, BF16), ("sres", D, F32), ("xcp", 16 * 131, F32),
                            ("convo", D, F32)]:
            dbg[nm] = nc.dram_tensor("dbg_" + nm, [128, w_], dt_,
                                     kind="ExternalOutput").ap()

    S = Sched()
    with ExitStack() as ctx:
        def sb(name, shape, dt_):
            return ctx.enter_context(nc.sbuf_tensor(name, shape, dt_))

        def psf(name, shape, dt_):
            return ctx.enter_context(nc.psum_tensor(name, shape, dt_))

        WIN = [sb(f"win{k}", [128, 2 * DI], BF16) for k in range(NKD)]
        WOUT = [sb(f"wout{k}", [128, D], BF16) for k in range(NCH)]
        KQ = [sb(f"kq{k}", [128, D], BF16) for k in range(NKD)]
        WXT = sb("wxt", [128, 16 * 96], BF16)
        WDT = sb("wdt", [64, DI], BF16)
        GAM = sb("gam", [128, D], F32)
        FBB = sb("fbb", [128, D], F32)
        CW = sb("cw", [128, 64], F32)
        CBIA = sb("cbia", [128, 16], F32)
        BDT = sb("bdt", [128, 16], F32)
        SC = sb("scpool", [128, 32], F32)
        ONE1 = sb("one1", [1, 128], F32)
        ONEC = sb("onec", [128, 1], F32)
        IDT = sb("idt", [128, 128], BF16)
        T11 = sb("t11", [1, 8], F32)

        (C_SS, C_SWIN, C_SX, C_SDT, C_SWOUT, C_RSIN, C_RSK, C_CFFN,
         C_RSW, C_RSX, C_RSDT, C_RSWOUT, C_EPS, C_ONE, C_NEG1,
         C_mLN2, C_LN2, C_7LN2, C_m7LN2, C_15LN2, C_8LN2, C_6LN2) = range(22)

        XT = sb("xt", [128, D], F32)
        XN = sb("xn", [128, D], F32)
        SA = sb("sa", [128, D], F32)
        SB_ = sb("sbb", [128, D], F32)
        SD = sb("sd", [128, D], F32)
        SG = sb("sg", [128, D], F32)
        GBUF = sb("gbuf", [128, D], F32)
        SRES = sb("sres", [128, D], F32)
        MSKA = sb("mska", [128, D], U8)
        XNB = sb("xnb", [128, D], BF16)
        XQB = sb("xqb", [128, D], BF16)
        XNB_FM = sb("xnbfm", [128, D], BF16)
        XQ_FM = sb("xqfm", [128, D], BF16)
        COLS = sb("colsp", [128, 16], F32)
        XCP = sb("xcp", [128, 16 * 131], F32)
        HALO = sb("halo", [128, 48], F32)
        Z = sb("zbuf", [128, DI], F32)
        XC = sb("xcbuf", [128, DI], F32)
        XCB = sb("xcbb", [128, DI], BF16)
        DLB = sb("dlb", [64, 128], BF16)
        Y = sb("ybuf", [128, DI], BF16)
        STATE = sb("state", [128, 16], F32)
        T1 = sb("tt1", [128, 128], F32)
        T2 = sb("tt2", [128, 128], F32)

        Q1 = XC[:, 0:D]
        Q2 = XC[:, D:2 * D]
        Q3 = Z[:, 0:D]
        SE = Z[:, D:2 * D]
        GROW = SA[0:1, 0:D]
        FBROWF = SB_[0:1, 0:D]
        FBROW = XNB[0:1, 0:D]
        DTBUF = XCP[:, 0:DI]

        PT = psf("pt", [128, 128], BF16)
        PT2 = psf("pt2", [128, 128], BF16)
        PA = psf("pa", [128, 128], F32)
        PB = psf("pb", [128, 128], F32)
        PF0 = psf("pf0", [128, 512], F32)
        PF1 = psf("pf1", [128, 512], F32)
        PX = psf("px", [96, 128], F32)
        PD = psf("pd", [128, 128], F32)

        sems = {e: ctx.enter_context(nc.semaphore(f"sem_{e}")) for e in ENGINES}
        block = ctx.enter_context(nc.Block())

        col = lambda tile, i: tile[:, i:i + 1]
        sq_ = S.seq

        def dma(dst, src, after=(), chain=True):
            return sq_("sync", lambda e, d=dst, s=src: e.dma_start(out=d, in_=s),
                       after=after, chain=chain)

        def V(fn, after=(), lag=0, chain=True):
            return sq_("vector", fn, after=after, lag=lag, chain=chain)

        def A(fn, after=(), lag=0, chain=True):
            return sq_("scalar", fn, after=after, lag=lag, chain=chain)

        def T(fn, after=(), chain=True, sig=False):
            return sq_("tensor", fn, after=after, chain=chain, sig=sig)

        def G(fn, after=(), lag=0, chain=True):
            return sq_("gpsimd", fn, after=after, lag=lag, chain=chain)

        def rne_s(src, t1, t2, out, msk=None):
            m = msk if msk is not None else MSKA[:]
            V(lambda e: e.tensor_scalar(t1, src, P23, P23, ALU.add, ALU.subtract))
            V(lambda e: e.tensor_scalar(t2, t1, P52, P52, ALU.add, ALU.subtract))
            V(lambda e: e.tensor_scalar(m, t2, 0.0, None, ALU.is_equal))
            return V(lambda e: e.select(out, m, t1, t2))

        def exact_pow2(buf):
            return V(lambda e: e.tensor_scalar(buf, buf, P23, P23,
                                               ALU.add, ALU.subtract))

        def f32z_s(w32, t2, out, msk=None):
            m = msk if msk is not None else MSKA[:]
            V(lambda e: e.tensor_scalar(t2, w32, P52, P52, ALU.add, ALU.subtract))
            V(lambda e: e.tensor_scalar(m, t2, 0.0, None, ALU.is_equal))
            return V(lambda e: e.select(out, m, w32, t2))

        G(lambda e: e.memset(ONE1[:], 1.0))
        G(lambda e: e.memset(ONEC[:], 1.0))
        G(lambda e: e.memset(IDT[:], 0.0))
        G(lambda e: e.affine_select(
            out=IDT[:], in_=IDT[:], compare_op=ALU.not_equal, fill=1.0,
            base=0, pattern=[[-1, 128]], channel_multiplier=1))
        for cc, val in [(C_EPS, 1e-5), (C_ONE, 1.0), (C_NEG1, -1.0),
                        (C_mLN2, -LN2), (C_LN2, LN2), (C_7LN2, 7 * LN2),
                        (C_m7LN2, -7 * LN2), (C_15LN2, 15 * LN2),
                        (C_8LN2, 8 * LN2), (C_6LN2, 6 * LN2)]:
            G(lambda e, c=cc, v=val: e.memset(col(SC, c), v))

        def bcast11(src11, dst_col):
            T(lambda e, s=src11: e.matmul(PF0[:, 0:1], ONE1[:], s,
                                          start=True, stop=True))
            V(lambda e, d=dst_col: e.tensor_copy(d, PF0[:, 0:1]))

        def xpart_sum(col_ap, npart, dst11):
            T(lambda e, c=col_ap, n=npart: e.matmul(
                PD[0:1, 0:1], c, ONEC[0:n, :], start=True, stop=True))
            V(lambda e, d=dst11: e.tensor_copy(d, PD[0:1, 0:1]))

        dma(T11[0:1, 0:1], ss_h[:])
        dma(T11[0:1, 1:2], s_in_h[:])
        dma(T11[0:1, 2:3], s_k_h[:])
        V(lambda e: e.tensor_tensor(T11[0:1, 3:4], T11[0:1, 1:2], T11[0:1, 2:3], ALU.mult))
        V(lambda e: e.reciprocal(T11[0:1, 4:5], T11[0:1, 1:2]))
        V(lambda e: e.reciprocal(T11[0:1, 5:6], T11[0:1, 2:3]))
        bcast11(T11[0:1, 0:1], col(SC, C_SS))
        bcast11(T11[0:1, 3:4], col(SC, C_CFFN))
        bcast11(T11[0:1, 4:5], col(SC, C_RSIN))
        bcast11(T11[0:1, 5:6], col(SC, C_RSK))

        dma(GROW, gamma_h[:])
        for h in range(2):
            T(lambda e, h=h: e.matmul(PF0[:, 0:512], ONE1[:],
                                      SA[0:1, h * 512:(h + 1) * 512],
                                      start=True, stop=True))
            V(lambda e, h=h: e.tensor_copy(GAM[:, h * 512:(h + 1) * 512], PF0[:, 0:512]))
        dma(FBROW, fbias_h[:])
        V(lambda e: e.tensor_copy(FBROWF, FBROW))
        for h in range(2):
            T(lambda e, h=h: e.matmul(PF0[:, 0:512], ONE1[:],
                                      SB_[0:1, h * 512:(h + 1) * 512],
                                      start=True, stop=True))
            V(lambda e, h=h: e.tensor_copy(FBB[:, h * 512:(h + 1) * 512], PF0[:, 0:512]))

        for c in range(16):
            dma(CW[:, c * 4:(c + 1) * 4], conv_w_h[c * 128:(c + 1) * 128, :])
            dma(col(CBIA, c), conv_b_h[c * 128:(c + 1) * 128, :])
            dma(col(BDT, c), b_dt_h[c * 128:(c + 1) * 128, :])

        def quantize_weight(dram, nchunks, width, rsw_col, s_dst_col, rs_dst_col,
                            nelem, chunkbuf, sgnbuf, mskbuf, qbuf_bf16,
                            after_quant_chunk):
            for c in range(nchunks):
                dma(chunkbuf[:, 0:width], dram[c * 128:(c + 1) * 128, :])
                V(lambda e: e.reduce_sum(col(COLS, 0), chunkbuf[:, 0:width],
                                         AXX, apply_absolute_value=True))
                if c == 0:
                    V(lambda e: e.tensor_copy(col(COLS, 1), col(COLS, 0)))
                else:
                    V(lambda e: e.tensor_tensor(col(COLS, 1), col(COLS, 1),
                                                col(COLS, 0), ALU.add))
            xpart_sum(col(COLS, 1), 128, T11[0:1, 6:7])
            V(lambda e, n=nelem: e.tensor_scalar(T11[0:1, 6:7], T11[0:1, 6:7],
                                                 1.0 / n, 1e-8, ALU.mult, ALU.add))
            V(lambda e: e.reciprocal(T11[0:1, 7:8], T11[0:1, 6:7]))
            bcast11(T11[0:1, 6:7], s_dst_col)
            bcast11(T11[0:1, 7:8], rs_dst_col)
            for c in range(nchunks):
                dma(chunkbuf[:, 0:width], dram[c * 128:(c + 1) * 128, :])
                A(lambda e: e.activation(sgnbuf[:, 0:width], chunkbuf[:, 0:width], AF.Sign))
                V(lambda e, rc=rsw_col: e.tensor_scalar(
                    mskbuf[:, 0:width], chunkbuf[:, 0:width], rc, None, ALU.mult))
                A(lambda e: e.activation(mskbuf[:, 0:width], mskbuf[:, 0:width], AF.Abs))
                V(lambda e: e.tensor_scalar(mskbuf[:, 0:width], mskbuf[:, 0:width],
                                            0.5, None, ALU.is_gt))
                V(lambda e: e.tensor_tensor(qbuf_bf16[:, 0:width], sgnbuf[:, 0:width],
                                            mskbuf[:, 0:width], ALU.mult))
                after_quant_chunk(c)

        def win_chunk(c):
            for k in range(NKD):
                T(lambda e, k=k: e.transpose(PT[:], XNB[:, k * 128:(k + 1) * 128], IDT[:]))
                V(lambda e, k=k, c=c: e.tensor_copy(
                    WIN[k][:, c * 128:(c + 1) * 128], PT[:]))
        quantize_weight(w_in_h, 32, D, col(SC, C_RSW), col(SC, C_SWIN),
                        col(SC, C_RSW), 4096 * 1024, XT, SA, SB_, XNB, win_chunk)

        def wout_chunk(c):
            for k in range(NCH):
                T(lambda e, k=k: e.transpose(PT[:], XCB[:, k * 128:(k + 1) * 128], IDT[:]))
                V(lambda e, k=k, c=c: e.tensor_copy(
                    WOUT[k][:, c * 128:(c + 1) * 128], PT[:]))
        quantize_weight(w_out_h, 8, DI, col(SC, C_RSWOUT), col(SC, C_SWOUT),
                        col(SC, C_RSWOUT), 1024 * 2048, Z, XCP[:, 0:DI],
                        XC[:, 0:DI], XCB, wout_chunk)

        def wdt_chunk(c):
            T(lambda e: e.transpose(PT[0:64, :], XNB[:, 0:64], IDT[:]))
            V(lambda e, c=c: e.tensor_copy(WDT[:, c * 128:(c + 1) * 128], PT[0:64, :]))
        quantize_weight(w_dt_h, 16, 64, col(SC, C_RSDT), col(SC, C_SDT),
                        col(SC, C_RSDT), 2048 * 64, XT, SA, SB_, XNB, wdt_chunk)

        dma(XCP[0:96, 0:DI], w_x_h[:])
        V(lambda e: e.reduce_sum(COLS[0:96, 0:1], XCP[0:96, 0:DI], AXX,
                                 apply_absolute_value=True))
        xpart_sum(COLS[0:96, 0:1], 96, T11[0:1, 6:7])
        V(lambda e: e.tensor_scalar(T11[0:1, 6:7], T11[0:1, 6:7],
                                    1.0 / (96 * 2048), 1e-8, ALU.mult, ALU.add))
        V(lambda e: e.reciprocal(T11[0:1, 7:8], T11[0:1, 6:7]))
        bcast11(T11[0:1, 6:7], col(SC, C_SX))
        bcast11(T11[0:1, 7:8], col(SC, C_RSX))
        A(lambda e: e.activation(Z[0:96, 0:DI], XCP[0:96, 0:DI], AF.Sign))
        V(lambda e: e.tensor_scalar(XC[0:96, 0:DI], XCP[0:96, 0:DI],
                                    SC[0:96, C_RSX:C_RSX + 1], None, ALU.mult))
        A(lambda e: e.activation(XC[0:96, 0:DI], XC[0:96, 0:DI], AF.Abs))
        V(lambda e: e.tensor_scalar(XC[0:96, 0:DI], XC[0:96, 0:DI], 0.5, None, ALU.is_gt))
        V(lambda e: e.tensor_tensor(XCB[0:96, 0:DI], Z[0:96, 0:DI],
                                    XC[0:96, 0:DI], ALU.mult))
        for c in range(16):
            T(lambda e, c=c: e.transpose(PT[:, 0:96], XCB[0:96, c * 128:(c + 1) * 128],
                                         IDT[0:96, 0:96]))
            V(lambda e, c=c: e.tensor_copy(WXT[:, c * 96:(c + 1) * 96], PT[:, 0:96]))

        def quant_fp8(vbuf, outbuf_bf16):
            A(lambda e: e.activation(SD[:], vbuf, AF.Abs))
            V(lambda e: e.tensor_scalar_max(SD[:], SD[:], 1e-20))
            A(lambda e: e.activation(SE, SD[:], AF.Ln))
            V(lambda e: e.tensor_scalar(SE, SE, RLN2, 255.5, ALU.mult, ALU.add))
            rne_s(SE, SD[:], SG[:], SE)
            V(lambda e: e.tensor_scalar(SE, SE, -256.0, -6.0, ALU.add, ALU.max))
            V(lambda e: e.tensor_scalar_min(SE, SE, 8.0))
            A(lambda e: e.activation(SD[:], SE, AF.Exp,
                                     bias=col(SC, C_8LN2), scale=col(SC, C_mLN2)))
            exact_pow2(SD[:])
            V(lambda e: e.tensor_tensor(SG[:], vbuf, SD[:], ALU.mult))
            V(lambda e: e.tensor_scalar(SG[:], SG[:], float(2.0 ** 15), None, ALU.mult))
            f32z_s(SG[:], SRES[:], GBUF[:])
            V(lambda e: e.tensor_scalar(GBUF[:], GBUF[:], float(2.0 ** -20), 1024.0,
                                        ALU.mult, ALU.add))
            rne_s(GBUF[:], SG[:], SRES[:], GBUF[:])
            A(lambda e: e.activation(SD[:], SE, AF.Exp,
                                     bias=col(SC, C_6LN2), scale=col(SC, C_LN2)))
            exact_pow2(SD[:])
            V(lambda e: e.tensor_scalar(SD[:], SD[:], float(2.0 ** -9), None, ALU.mult))
            V(lambda e: e.scalar_tensor_tensor(SRES[:], GBUF[:], -1024.0, SD[:],
                                               ALU.add, ALU.mult))
            V(lambda e: e.tensor_copy(outbuf_bf16, SRES[:]))

        for k in range(NKD):
            dma(XQB[:], kern_h[k * 128:(k + 1) * 128, :])
            V(lambda e: e.tensor_copy(SA[:], XQB[:]))
            V(lambda e: e.tensor_scalar(SA[:], SA[:], col(SC, C_RSK), None, ALU.mult))
            quant_fp8(SA[:], XQB[:])
            V(lambda e, k=k: e.tensor_copy(KQ[k][:], XQB[:]))

        def soft_bf16_x2(src, dst):
            A(lambda e: e.activation(SA[:], src, AF.Abs))
            V(lambda e: e.tensor_scalar_max(SA[:], SA[:], 1e-20))
            A(lambda e: e.activation(SB_[:], SA[:], AF.Ln))
            V(lambda e: e.tensor_scalar(SB_[:], SB_[:], RLN2, 255.5, ALU.mult, ALU.add))
            rne_s(SB_[:], SD[:], SA[:], SE)
            V(lambda e: e.tensor_scalar(SE, SE, -256.0, -15.0, ALU.add, ALU.max))
            V(lambda e: e.tensor_scalar_min(SE, SE, 6.0))
            A(lambda e: e.activation(SA[:], SE, AF.Exp,
                                     bias=col(SC, C_mLN2), scale=col(SC, C_mLN2)))
            exact_pow2(SA[:])
            A(lambda e: e.activation(SB_[:], SE, AF.Exp,
                                     bias=col(SC, C_7LN2), scale=col(SC, C_mLN2)))
            exact_pow2(SB_[:])
            V(lambda e: e.tensor_scalar(MSKA[:], SE, -7.5, None, ALU.is_ge))
            V(lambda e: e.tensor_scalar(SA[:], SA[:], 256.0, None, ALU.mult))
            V(lambda e: e.select(SD[:], MSKA[:], SB_[:], SA[:]))
            V(lambda e: e.tensor_tensor(Q1, src, SD[:], ALU.mult))
            V(lambda e: e.tensor_scalar(Q1, Q1, 65536.0, None, ALU.mult))
            f32z_s(Q1, Q2, Q3)
            V(lambda e: e.tensor_scalar(Q3, Q3, float(2.0 ** -16), 1024.0,
                                        ALU.mult, ALU.add))
            rne_s(Q3, Q1, Q2, Q3)
            A(lambda e: e.activation(SA[:], SE, AF.Exp,
                                     bias=col(SC, C_15LN2), scale=col(SC, C_LN2)))
            exact_pow2(SA[:])
            V(lambda e: e.tensor_scalar(SA[:], SA[:], float(2.0 ** -22), None, ALU.mult))
            A(lambda e: e.activation(SB_[:], SE, AF.Exp,
                                     bias=col(SC, C_7LN2), scale=col(SC, C_LN2)))
            exact_pow2(SB_[:])
            V(lambda e: e.tensor_scalar(SB_[:], SB_[:], float(2.0 ** -14), None, ALU.mult))
            V(lambda e: e.tensor_scalar(MSKA[:], SE, -1.5, None, ALU.is_ge))
            V(lambda e: e.select(SD[:], MSKA[:], SB_[:], SA[:]))
            V(lambda e: e.scalar_tensor_tensor(dst, Q3, -1024.0, SD[:],
                                               ALU.add, ALU.mult))

        def soft_bf16_wide(src, dst, ta, tb, tc, td):
            A(lambda e: e.activation(ta, src, AF.Abs))
            V(lambda e: e.tensor_scalar_max(ta, ta, 1e-30))
            A(lambda e: e.activation(tb, ta, AF.Ln))
            V(lambda e: e.tensor_scalar(tb, tb, RLN2, 255.5, ALU.mult, ALU.add))
            rne_s(tb, ta, tc, tb)
            V(lambda e: e.tensor_scalar(tb, tb, -256.0, -30.0, ALU.add, ALU.max))
            V(lambda e: e.tensor_scalar_min(tb, tb, 12.0))
            A(lambda e: e.activation(ta, tb, AF.Exp,
                                     bias=col(SC, C_7LN2), scale=col(SC, C_mLN2)))
            V(lambda e: e.tensor_tensor(tc, src, ta, ALU.mult))
            V(lambda e: e.tensor_scalar(tc, tc, 65536.0, None, ALU.mult))
            f32z_s(tc, ta, td)
            V(lambda e: e.tensor_scalar(td, td, float(2.0 ** -16), 1024.0,
                                        ALU.mult, ALU.add))
            rne_s(td, ta, tc, td)
            A(lambda e: e.activation(ta, tb, AF.Exp,
                                     bias=col(SC, C_m7LN2), scale=col(SC, C_LN2)))
            V(lambda e: e.scalar_tensor_tensor(dst, td, -1024.0, ta,
                                               ALU.add, ALU.mult))

        def sp_half(xin, dst, ta, tb, tc):
            A(lambda e: e.activation(ta, xin, AF.Square))
            V(lambda e: e.tensor_scalar(ta, ta, 4.0, 1e6, ALU.add, ALU.min))
            V(lambda e: e.tensor_scalar(tb, ta, 16.0, -0.375, ALU.is_gt, ALU.mult))
            V(lambda e: e.tensor_scalar(tc, ta, 64.0, -0.0625, ALU.is_gt, ALU.mult))
            V(lambda e: e.scalar_tensor_tensor(tb, tb, 0.5, tc, ALU.add, ALU.add))
            V(lambda e: e.tensor_scalar(tc, ta, 256.0, -0.03125, ALU.is_gt, ALU.mult))
            V(lambda e: e.tensor_tensor(tb, tb, tc, ALU.add))
            for it in range(3):
                V(lambda e: e.tensor_tensor(tc, tb, tb, ALU.mult))
                V(lambda e: e.scalar_tensor_tensor(tc, tc, -0.5, ta, ALU.mult, ALU.mult))
                V(lambda e: e.scalar_tensor_tensor(tb, tc, 1.5, tb, ALU.add, ALU.mult))
                if it == 0:
                    V(lambda e: e.tensor_scalar_max(tb, tb, 1e-6))
            V(lambda e: e.scalar_tensor_tensor(tc, ta, 0.5, tb, ALU.mult, ALU.mult))
            V(lambda e: e.scalar_tensor_tensor(dst, xin, 0.5, tc, ALU.mult, ALU.add))

        vec_now = S.now("vector")
        ev_tok = [vec_now, vec_now]
        war_prev = S.now("scalar")

        xcp3 = XCP[:].rearrange("p (c w) -> p c w", w=131)
        halo3 = HALO[:].rearrange("p (c w) -> p c w", w=3)

        for tile in range(n_tiles):
            seq_start = (tile % seq_tiles) == 0
            r0 = tile * TOK

            dma(XT[:], x_h[r0:r0 + TOK, :])

            A(lambda e: e.activation(SA[:], XT[:], AF.Square, accum_out=col(COLS, 0)))
            A(lambda e: e.activation(SA[:], XT[:], AF.Identity, accum_out=col(COLS, 1)))
            V(lambda e: e.tensor_scalar(col(COLS, 2), col(COLS, 1), 1.0 / D, None, ALU.mult))
            V(lambda e: e.tensor_scalar(col(COLS, 3), col(COLS, 0), 1.0 / D, None, ALU.mult))
            V(lambda e: e.tensor_tensor(col(COLS, 4), col(COLS, 2), col(COLS, 2), ALU.mult))
            V(lambda e: e.tensor_tensor(col(COLS, 5), col(COLS, 3), col(COLS, 4), ALU.subtract))
            A(lambda e: e.activation(col(COLS, 6), col(COLS, 5), AF.Sqrt,
                                     bias=col(SC, C_EPS)))
            V(lambda e: e.reciprocal(col(COLS, 7), col(COLS, 6)))
            V(lambda e: e.tensor_scalar(XN[:], XT[:], col(COLS, 2), col(COLS, 7),
                                        ALU.subtract, ALU.mult))

            A(lambda e: e.activation(SA[:], XN[:], AF.Identity, accum_out=col(COLS, 8)))
            V(lambda e: e.tensor_scalar(col(COLS, 9), col(COLS, 8), 1.0 / D, None, ALU.mult))
            V(lambda e: e.tensor_scalar(SA[:], XN[:], col(COLS, 9), None, ALU.subtract))
            A(lambda e: e.activation(SB_[:], SA[:], AF.Square, accum_out=col(COLS, 10)))
            V(lambda e: e.tensor_scalar(col(COLS, 11), col(COLS, 10), 1.0 / D, 1e-9,
                                        ALU.mult, ALU.add))
            for i, (thr, dlt) in enumerate(
                    [(0.0625, -2.0), (0.25, -1.0), (4.0, -0.5), (16.0, -0.25),
                     (64.0, -0.125), (256.0, -0.0625), (1024.0, -0.03125),
                     (4096.0, -0.015625), (16384.0, -0.0078125),
                     (65536.0, -0.00390625)]):
                V(lambda e, i=i, t_=thr, d_=dlt: e.tensor_scalar(
                    col(COLS, 4 + i), col(COLS, 11), t_, d_, ALU.is_ge, ALU.mult),
                  lag=i)
            V(lambda e: e.reduce_sum(col(COLS, 14), COLS[:, 4:14], AXX))
            V(lambda e: e.tensor_scalar(col(COLS, 14), col(COLS, 14), 4.0, None, ALU.add))
            V(lambda e: e.tensor_tensor(col(COLS, 14), col(COLS, 14),
                                        col(SC, C_SS), ALU.mult))
            V(lambda e: e.tensor_scalar(SA[:], SA[:], col(COLS, 14), None, ALU.mult))
            V(lambda e: e.tensor_tensor(XNB[:], SA[:], GAM[:], ALU.mult))

            soft_bf16_x2(XN[:], SG[:])
            if debug and tile == 0:
                dma(dbg["xn"], XN[:])
                dma(dbg["x2"], SG[:])
            V(lambda e: e.tensor_scalar(SA[:], SG[:], col(SC, C_RSIN), None, ALU.mult))
            quant_fp8(SA[:], XQB[:])
            if debug and tile == 0:
                dma(dbg["xqb"], XQB[:])

            vcur = S.now("vector")
            for idx in range(16):
                srcb, dstb = (XNB, XNB_FM) if idx < 8 else (XQB, XQ_FM)
                k = idx % 8
                pt = PT if idx % 2 == 0 else PT2
                ttok = T(lambda e, k=k, s=srcb, p=pt: e.transpose(
                    p[:], s[:, k * 128:(k + 1) * 128], IDT[:]),
                    after=[ev_tok[idx % 2], vcur], chain=False, sig=True)
                ev_tok[idx % 2] = A(lambda e, k=k, d=dstb, p=pt: e.activation(
                    d[:, k * 128:(k + 1) * 128], p[:], AF.Identity),
                    after=[ttok], chain=False)
            S.last_eng = "vector"

            ffn_tok = []
            for h, PF in ((0, PF0), (1, PF1)):
                tk = None
                for k in range(NKD):
                    tk = T(lambda e, k=k, h=h, PF=PF: e.matmul(
                        PF[:], XQ_FM[:, k * 128:(k + 1) * 128],
                        KQ[k][:, h * 512:(h + 1) * 512],
                        start=(k == 0), stop=(k == NKD - 1)),
                        after=([ev_tok[0], ev_tok[1]] if k == 0 else []),
                        chain=False, sig=(k == NKD - 1))
                ffn_tok.append(tk)
            for h, PF in ((0, PF0), (1, PF1)):
                V(lambda e, h=h, PF=PF: e.tensor_scalar(
                    SA[:, h * 512:(h + 1) * 512], PF[:], col(SC, C_CFFN), None, ALU.mult),
                    after=[ffn_tok[h]], chain=False)
            S.last_eng = "vector"
            soft_bf16_wide(SA[:], SD[:], SB_[:], SE, GBUF[:], SRES[:])
            V(lambda e: e.tensor_tensor(GBUF[:], SD[:], FBB[:], ALU.add))
            A(lambda e: e.activation(SG[:], GBUF[:], AF.Silu))
            if debug and tile == 0:
                dma(dbg["g"], SG[:])

            if seq_start:
                V(lambda e: e.memset(xcp3[:, :, 0:3], 0.0))
                hal_tok = V(lambda e: e.memset(STATE[:], 0.0))
            else:
                hal_tok = V(lambda e: e.tensor_copy(xcp3[:, :, 0:3], halo3[:, :, :]))
            ev_win = [None, None]
            for m in range(32):
                PM = PA if (m % 2 == 0) else PB
                aft = []
                if m == 0:
                    aft = [hal_tok, war_prev]
                elif ev_win[m % 2] is not None:
                    aft = [ev_win[m % 2]]
                tk = None
                for k in range(NKD):
                    tk = T(lambda e, m=m, k=k, PM=PM: e.matmul(
                        PM[:], WIN[k][:, m * 128:(m + 1) * 128],
                        XNB_FM[:, k * 128:(k + 1) * 128],
                        start=(k == 0), stop=(k == NKD - 1)),
                        after=(aft if k == 0 else []), chain=False,
                        sig=(k == NKD - 1))
                if m < 16:
                    ev_win[m % 2] = V(lambda e, m=m, PM=PM: e.tensor_scalar(
                        XCP[:, m * 131 + 3:m * 131 + 131], PM[:],
                        col(SC, C_SWIN), None, ALU.mult),
                        after=[tk], chain=False)
                else:
                    ev_win[m % 2] = V(lambda e, m=m, PM=PM: e.tensor_scalar(
                        Z[:, (m - 16) * 128:(m - 15) * 128], PM[:],
                        col(SC, C_SWIN), None, ALU.mult),
                        after=[tk], chain=False)
            S.last_eng = "vector"
            V(lambda e: e.tensor_copy(halo3[:, :, :], xcp3[:, :, 128:131]))
            if debug and tile == 0:
                dma(dbg["xcp"], XCP[:])

            for h in range(2):
                for j in range(8):
                    m = h * 8 + j
                    base = m * 131
                    V(lambda e, b=base, m=m: e.tensor_scalar(
                        T1[:], XCP[:, b:b + 128], col(CW, 4 * m), None, ALU.mult))
                    V(lambda e, b=base, m=m: e.scalar_tensor_tensor(
                        T2[:], XCP[:, b + 1:b + 129], col(CW, 4 * m + 1), T1[:],
                        ALU.mult, ALU.add))
                    V(lambda e, b=base, m=m: e.tensor_scalar(
                        T1[:], XCP[:, b + 3:b + 131], col(CW, 4 * m + 3), col(CBIA, m),
                        ALU.mult, ALU.add))
                    V(lambda e, b=base, m=m: e.scalar_tensor_tensor(
                        T1[:], XCP[:, b + 2:b + 130], col(CW, 4 * m + 2), T1[:],
                        ALU.mult, ALU.add))
                    V(lambda e, j=j: e.tensor_tensor(
                        SA[:, j * 128:(j + 1) * 128], T2[:], T1[:], ALU.add))
                if debug and tile == 0 and h == 0:
                    dma(dbg["convo"], SA[:])
                sp_half(SA[:], XC[:, h * D:(h + 1) * D], SB_[:], SD[:], GBUF[:])
            xcb_tok = V(lambda e: e.tensor_copy(XCB[:], XC[:]))
            if debug and tile == 0:
                dma(dbg["xc"], XC[:])
                dma(dbg["z"], Z[:])

            wx_tok = None
            for c in range(16):
                wx_tok = T(lambda e, c=c: e.matmul(
                    PX[:], WXT[:, c * 96:(c + 1) * 96], XCB[:, c * 128:(c + 1) * 128],
                    start=(c == 0), stop=(c == 15)),
                    after=([xcb_tok] if c == 0 else []), chain=False, sig=(c == 15))
            for h in range(2):
                zh = Z[:, h * D:(h + 1) * D]
                sp_half(zh, zh, SB_[:], SD[:], GBUF[:])
            dlb_tok = V(lambda e: e.tensor_scalar(
                DLB[:], PX[0:64, :], SC[0:64, C_SX:C_SX + 1], None, ALU.mult),
                after=[wx_tok], chain=False)
            S.last_eng = "vector"

            ev_dt = [dlb_tok, dlb_tok]
            for m in range(16):
                PM = PD if (m % 2 == 0) else PA
                tk = T(lambda e, m=m, PM=PM: e.matmul(
                    PM[:], WDT[:, m * 128:(m + 1) * 128], DLB[:],
                    start=True, stop=True), after=[ev_dt[m % 2]],
                    chain=False, sig=True)
                ev_dt[m % 2] = A(lambda e, m=m, PM=PM: e.activation(
                    DTBUF[:, m * 128:(m + 1) * 128], PM[:], AF.Identity,
                    bias=col(BDT, m), scale=col(SC, C_SDT)),
                    after=[tk], chain=False)
            S.last_eng = "scalar"
            war_prev = S.now("scalar")

            ylast = None
            for h in range(2):
                dth = DTBUF[:, h * D:(h + 1) * D]
                zh = Z[:, h * D:(h + 1) * D]
                A(lambda e, dth=dth: e.activation(SA[:], dth, AF.Abs))
                A(lambda e: e.activation(SB_[:], SA[:], AF.Ln, bias=col(SC, C_ONE)))
                A(lambda e: e.activation(SD[:], SB_[:], AF.Exp, scale=col(SC, C_NEG1)))
                V(lambda e: e.scalar_tensor_tensor(SB_[:], SA[:], 1.0, SD[:],
                                                   ALU.add, ALU.mult))
                V(lambda e: e.tensor_scalar(SB_[:], SB_[:], -1.0, 2.0, ALU.mult, ALU.add))
                V(lambda e: e.tensor_tensor(SD[:], SD[:], SB_[:], ALU.mult))
                V(lambda e, dth=dth: e.tensor_tensor(SA[:], dth, SD[:], ALU.mult))
                V(lambda e: e.tensor_scalar(SA[:], SA[:], 0.5, 0.5, ALU.mult, ALU.add))
                V(lambda e: e.tensor_scalar(SA[:], SA[:], 32768.0, 2.0, ALU.mult, ALU.add))
                rne_s(SA[:], SD[:], GBUF[:], SA[:])
                V(lambda e: e.tensor_scalar(SB_[:], SA[:], float(2.0 ** -15),
                                            float(-2.0 ** -14), ALU.mult, ALU.add))
                V(lambda e: e.tensor_scalar(GBUF[:], SA[:], float(-2.0 ** -15),
                                            1.00006103515625, ALU.mult, ALU.add))
                V(lambda e, h=h: e.tensor_tensor(SD[:], GBUF[:],
                                                 XC[:, h * D:(h + 1) * D], ALU.mult))
                for j in range(8):
                    m = h * 8 + j
                    V(lambda e, j=j, m=m: e.tensor_tensor_scan(
                        GBUF[:, j * 128:(j + 1) * 128],
                        SB_[:, j * 128:(j + 1) * 128],
                        SD[:, j * 128:(j + 1) * 128],
                        col(STATE, m), ALU.mult, ALU.add), lag=j)
                for j in range(8):
                    m = h * 8 + j
                    V(lambda e, j=j, m=m: e.tensor_copy(
                        col(STATE, m), GBUF[:, j * 128 + 127:j * 128 + 128]), lag=7)
                for j in range(8):
                    m = h * 8 + j
                    ylast = V(lambda e, j=j, m=m, zh=zh: e.tensor_tensor(
                        Y[:, m * 128:(m + 1) * 128], GBUF[:, j * 128:(j + 1) * 128],
                        zh[:, j * 128:(j + 1) * 128], ALU.mult), lag=7)
            if debug and tile == 0:
                dma(dbg["y"], Y[:])

            wo_tok = []
            for h, PF in ((0, PF0), (1, PF1)):
                tk = None
                for k in range(NCH):
                    tk = T(lambda e, k=k, h=h, PF=PF: e.matmul(
                        PF[:], Y[:, k * 128:(k + 1) * 128],
                        WOUT[k][:, h * 512:(h + 1) * 512],
                        start=(k == 0), stop=(k == NCH - 1)),
                        after=([ylast] if k == 0 else []),
                        chain=False, sig=(k == NCH - 1))
                wo_tok.append(tk)
            for h, PF in ((0, PF0), (1, PF1)):
                V(lambda e, h=h, PF=PF: e.scalar_tensor_tensor(
                    SRES[:, h * 512:(h + 1) * 512], PF[:], col(SC, C_SWOUT),
                    XN[:, h * 512:(h + 1) * 512], ALU.mult, ALU.add),
                    after=[wo_tok[h]], chain=False)
            S.last_eng = "vector"
            if debug and tile == 0:
                dma(dbg["sres"], SRES[:])

            V(lambda e: e.tensor_tensor(SA[:], SRES[:], SG[:], ALU.mult))
            V(lambda e: e.tensor_tensor(SA[:], SA[:], XT[:], ALU.add))
            dma(out_h[r0:r0 + TOK, :], SA[:])

        G(lambda e: e.engine_nop())

        @block.sync
        def _(sync):
            S.emit(sync, "sync", sems)

        @block.tensor
        def _(tensor):
            S.emit(tensor, "tensor", sems)

        @block.vector
        def _(vector):
            S.emit(vector, "vector", sems)

        @block.scalar
        def _(scalar):
            S.emit(scalar, "scalar", sems)

        @block.gpsimd
        def _(gpsimd):
            S.emit(gpsimd, "gpsimd", sems)

    return nc


_NC_CACHE = {}


def _get_nc(n_tiles, seq_tiles):
    key = (n_tiles, seq_tiles)
    if key not in _NC_CACHE:
        _NC_CACHE[key] = build_nc(n_tiles, seq_tiles)
    return _NC_CACHE[key]


def _in_map(x_shard, inputs):
    f32 = lambda a: np.ascontiguousarray(np.asarray(a), dtype=np.float32)
    return {
        "x": f32(x_shard).reshape(-1, D),
        "gamma": f32(inputs["L0_attn_gamma"]).reshape(1, D),
        "step_size": f32(inputs["L0_attn_step_size"]).reshape(1, 1),
        "w_in": f32(inputs["L0_attn_W_in"]),
        "conv_w": f32(inputs["L0_attn_conv_w"]).reshape(DI, 4),
        "conv_b": f32(inputs["L0_attn_conv_b"]).reshape(DI, 1),
        "w_x": f32(inputs["L0_attn_W_x"]),
        "w_dt": f32(inputs["L0_attn_W_dt"]),
        "b_dt": f32(inputs["L0_attn_b_dt"]).reshape(DI, 1),
        "w_out": f32(inputs["L0_attn_W_out"]),
        "ffn_kernel": np.ascontiguousarray(
            np.asarray(inputs["L0_ffn_kernel"]), dtype=ml_dtypes.bfloat16),
        "ffn_input_scale": f32(inputs["L0_ffn_input_scale"]).reshape(1, 1),
        "ffn_kernel_scale": f32(inputs["L0_ffn_kernel_scale"]).reshape(1, 1),
        "ffn_bias": np.ascontiguousarray(
            np.asarray(inputs["L0_ffn_bias"]), dtype=ml_dtypes.bfloat16).reshape(1, D),
    }


def kernel(**inputs):
    x = np.asarray(inputs["L0_attn_x"])
    B, L, _ = x.shape
    per = B // N_CORES
    nc = _get_nc((per * L) // TOK, L // TOK)
    in_maps = [_in_map(x[i * per:(i + 1) * per], inputs) for i in range(N_CORES)]
    res = run_bass_kernel_spmd(nc, in_maps, core_ids=list(range(N_CORES)))
    outs = [np.asarray(r["out"], dtype=np.float32).reshape(per, L, D)
            for r in res.results]
    return np.concatenate(outs, axis=0)


# version 06
# speedup vs baseline: 1.8603x; 1.0196x over previous
import numpy as np
import ml_dtypes
import concourse.bass as bass
import concourse.mybir as mybir
from concourse.bass_utils import run_bass_kernel_spmd
from contextlib import ExitStack

F32 = mybir.dt.float32
BF16 = mybir.dt.bfloat16
U8 = mybir.dt.uint8
ALU = mybir.AluOpType
AF = mybir.ActivationFunctionType
AXX = mybir.AxisListType.X

LN2 = float(np.log(2.0))
RLN2 = float(1.0 / np.log(2.0))
P23 = float(2.0 ** 23)
P52 = float(2.0 ** 52)

D = 1024
DI = 2048
TOK = 128
NKD = D // 128
NCH = DI // 128
N_CORES = 8

ENGINES = ("sync", "tensor", "vector", "scalar", "gpsimd")
VSG = ("vector", "scalar", "gpsimd")


def _ref_rne(i0, i1, s0, s1, imm2):
    t1 = (i0 + s0) - s0
    t2 = (t1 + s1) - s1
    return np.where(t2 == 0, t1, t2)


def _ref_f32z(i0, i1, s0, s1, imm2):
    w = i0 * imm2
    t2 = (w + s1) - s1
    return np.where(t2 == 0, w, t2)


def _ref_sqy(i0, i1, s0, s1, imm2):
    return np.minimum(i0 * i0 + 4.0, s0)


def _ref_sqr0a(i0, i1, s0, s1, imm2):
    return np.where(i0 > s0 * 4.0, s1 * imm2,
                    np.where(i0 > s0, s1, imm2)).astype(i0.dtype)


def _ref_sqr0b(i0, i1, s0, s1, imm2):
    return np.where(i0 > s0, np.asarray(s1, dtype=i0.dtype), i1)


def _ref_sqit1(i0, i1, s0, s1, imm2):
    return np.maximum(((i0 * i0) * i1 * s0 + s1) * i0, imm2)


def _ref_sqit(i0, i1, s0, s1, imm2):
    return ((i0 * i0) * i1 * s0 + s1) * i0


def _ref_dtq(i0, i1, s0, s1, imm2):
    av = np.maximum(i0, 0.0 - i0)
    t = (av + 1.0) * i1
    t = 2.0 - t
    t = i1 * t
    return i0 * t


def _ref_dtb(i0, i1, s0, s1, imm2):
    return (i0 * s0 + s1) * i1


_OPS = {}


def _register_custom_ops():
    if _OPS:
        return _OPS
    import concourse.dve_ops as dom
    from concourse.dve_op import DveOp
    from concourse.dve_spec import (Spec, Src0, Src1, C0, C1, C2, Zero, One,
                                    select, eq, sq, maxx, minn, lower)
    from concourse.dve_table_gen import dve_ver_for
    ver = dve_ver_for("TRN2")
    K2 = One + One
    K4 = K2 * K2

    def reg(name, body, ref, row, perf=False):
        existing = next((o for o in dom.OPS if o.name == name), None)
        if existing is not None:
            _OPS[name] = existing
            return
        spec = Spec(body=body)
        try:
            low = lower(spec, ver=ver, perf_en=perf, name=name)
        except Exception:
            perf = False
            low = lower(spec, ver=ver, perf_en=False, name=name)
        op = DveOp(name, spec, subdim=False, uops_sha={ver: low.sha(ver)},
                   perf_en={ver: perf} if perf else {}, reference=ref)
        dom.OPS.append(op)
        dom.CUSTOM_DVE_SPECS[name] = spec
        dom.CUSTOM_DVE_REFS[name] = ref
        dom._SUB_OPCODE_FOR_NAME[name] = row
        _OPS[name] = op

    t1 = (Src0 + C0) - C0
    t2 = (t1 + C1) - C1
    reg("ANT_RNE", select(eq(t2, Zero), t1, t2), _ref_rne, 1)
    w = Src0 * C2
    tw = (w + C1) - C1
    reg("ANT_F32Z", select(eq(tw, Zero), w, tw), _ref_f32z, 2)
    reg("ANT_SQY", minn(sq(Src0) + K4, C0), _ref_sqy, 3)
    reg("ANT_SQR0A",
        select(Src0 > C0 * K4, C1 * C2, select(Src0 > C0, C1, C2)),
        _ref_sqr0a, 4)
    reg("ANT_SQR0B", select(Src0 > C0, C1, Src1), _ref_sqr0b, 5)
    reg("ANT_SQIT1", maxx(((sq(Src0) * Src1) * C0 + C1) * Src0, C2),
        _ref_sqit1, 6)
    reg("ANT_SQIT", ((sq(Src0) * Src1) * C0 + C1) * Src0, _ref_sqit, 7)
    av = maxx(Src0, Zero - Src0)
    reg("ANT_DTQ", Src0 * (Src1 * (K2 - (av + One) * Src1)), _ref_dtq, 8)
    reg("ANT_DTB", (Src0 * C0 + C1) * Src1, _ref_dtb, 9)
    return _OPS


class Sched:

    def __init__(self):
        self.plan = {e: [] for e in ENGINES}
        self.counts = {e: 0 for e in ENGINES}
        self.last_eng = None

    def _ensure_sig(self, eng):
        fn0, deps0, sig0 = self.plan[eng][-1]
        if sig0 == 0:
            amt = 16 if eng == "sync" else 1
            self.plan[eng][-1] = (fn0, deps0, amt)
            self.counts[eng] += amt
        return (eng, self.counts[eng])

    def seq(self, eng, fn, after=(), lag=0, chain=True, sig=False):
        deps = {}
        for t in after:
            if t is None:
                continue
            e2, vv = t
            if e2 != eng or eng not in VSG:
                deps[e2] = max(deps.get(e2, 0), vv)
        if chain and self.last_eng is not None and self.last_eng != eng:
            le = self.last_eng
            tok = self._ensure_sig(le)
            deps[le] = max(deps.get(le, 0), tok[1])
        if eng in VSG:
            own = self.counts[eng] - lag
            for t in after:
                if t is not None and t[0] == eng:
                    own = max(own, t[1])
            if own > 0:
                deps[eng] = max(deps.get(eng, 0), own)
            self.plan[eng].append((fn, tuple(deps.items()), 1))
            self.counts[eng] += 1
        elif eng == "sync":
            self.plan[eng].append((fn, tuple(deps.items()), 16))
            self.counts["sync"] += 16
        else:
            amt = 1 if sig else 0
            self.plan[eng].append((fn, tuple(deps.items()), amt))
            self.counts[eng] += amt
        if chain:
            self.last_eng = eng
        return (eng, self.counts[eng])

    def now(self, eng):
        return (eng, self.counts[eng])

    def emit(self, engine_obj, eng_name, sems):
        for fn, deps, sig in self.plan[eng_name]:
            for (e2, vv) in deps:
                engine_obj.wait_ge(sems[e2], vv)
            inst = fn(engine_obj)
            if sig:
                inst.then_inc(sems[eng_name], sig)


def build_nc(n_tiles=64, seq_tiles=16, debug=False):
    nc = bass.Bass("TRN2", target_bir_lowering=False)
    n_rows = n_tiles * TOK

    def din(name, shape, dt_):
        return nc.dram_tensor(name, shape, dt_, kind="ExternalInput").ap()

    x_h = din("x", [n_rows, D], F32)
    gamma_h = din("gamma", [1, D], F32)
    ss_h = din("step_size", [1, 1], F32)
    w_in_h = din("w_in", [2 * DI, D], F32)
    conv_w_h = din("conv_w", [DI, 4], F32)
    conv_b_h = din("conv_b", [DI, 1], F32)
    w_x_h = din("w_x", [96, DI], F32)
    w_dt_h = din("w_dt", [DI, 64], F32)
    b_dt_h = din("b_dt", [DI, 1], F32)
    w_out_h = din("w_out", [D, DI], F32)
    kern_h = din("ffn_kernel", [D, D], BF16)
    s_in_h = din("ffn_input_scale", [1, 1], F32)
    s_k_h = din("ffn_kernel_scale", [1, 1], F32)
    fbias_h = din("ffn_bias", [1, D], BF16)
    out_h = nc.dram_tensor("out", [n_rows, D], F32, kind="ExternalOutput").ap()
    dbg = {}
    if debug:
        for nm, w_, dt_ in [("xn", D, F32), ("x2", D, F32), ("xqb", D, BF16),
                            ("g", D, F32), ("z", DI, F32), ("xc", DI, F32),
                            ("y", DI, BF16), ("sres", D, F32), ("xcp", 16 * 131, F32),
                            ("convo", D, F32)]:
            dbg[nm] = nc.dram_tensor("dbg_" + nm, [128, w_], dt_,
                                     kind="ExternalOutput").ap()

    S = Sched()
    with ExitStack() as ctx:
        def sb(name, shape, dt_):
            return ctx.enter_context(nc.sbuf_tensor(name, shape, dt_))

        def psf(name, shape, dt_):
            return ctx.enter_context(nc.psum_tensor(name, shape, dt_))

        WIN = [sb(f"win{k}", [128, 2 * DI], BF16) for k in range(NKD)]
        WOUT = [sb(f"wout{k}", [128, D], BF16) for k in range(NCH)]
        KQ = [sb(f"kq{k}", [128, D], BF16) for k in range(NKD)]
        WXT = sb("wxt", [128, 16 * 96], BF16)
        WDT = sb("wdt", [64, DI], BF16)
        GAM = sb("gam", [128, D], F32)
        FBB = sb("fbb", [128, D], F32)
        CW = sb("cw", [128, 64], F32)
        CBIA = sb("cbia", [128, 16], F32)
        BDT = sb("bdt", [128, 16], F32)
        SC = sb("scpool", [128, 32], F32)
        ONE1 = sb("one1", [1, 128], F32)
        ONEC = sb("onec", [128, 1], F32)
        IDT = sb("idt", [128, 128], BF16)
        T11 = sb("t11", [1, 8], F32)

        (C_SS, C_SWIN, C_SX, C_SDT, C_SWOUT, C_RSIN, C_RSK, C_CFFN,
         C_RSW, C_RSX, C_RSDT, C_RSWOUT, C_EPS, C_ONE, C_NEG1,
         C_mLN2, C_LN2, C_7LN2, C_m7LN2, C_15LN2, C_8LN2, C_6LN2) = range(22)

        XT = sb("xt", [128, D], F32)
        XN = sb("xn", [128, D], F32)
        SA = sb("sa", [128, D], F32)
        SB_ = sb("sbb", [128, D], F32)
        SD = sb("sd", [128, D], F32)
        SG = sb("sg", [128, D], F32)
        GBUF = sb("gbuf", [128, D], F32)
        SRES = sb("sres", [128, D], F32)
        MSKA = sb("mska", [128, D], U8)
        XNB = sb("xnb", [128, D], BF16)
        XQB = sb("xqb", [128, D], BF16)
        XNB_FM = sb("xnbfm", [128, D], BF16)
        XQ_FM = sb("xqfm", [128, D], BF16)
        COLS = sb("colsp", [128, 16], F32)
        XCP = sb("xcp", [128, 16 * 131], F32)
        HALO = sb("halo", [128, 48], F32)
        Z = sb("zbuf", [128, DI], F32)
        XC = sb("xcbuf", [128, DI], F32)
        XCB = sb("xcbb", [128, DI], BF16)
        DLB = sb("dlb", [64, 128], BF16)
        Y = sb("ybuf", [128, DI], BF16)
        STATE = sb("state", [128, 16], F32)
        T1 = sb("tt1", [128, 128], F32)
        T2 = sb("tt2", [128, 128], F32)

        Q1 = XC[:, 0:D]
        Q2 = XC[:, D:2 * D]
        Q3 = Z[:, 0:D]
        SE = Z[:, D:2 * D]
        GROW = SA[0:1, 0:D]
        FBROWF = SB_[0:1, 0:D]
        FBROW = XNB[0:1, 0:D]
        DTBUF = XCP[:, 0:DI]

        PT = psf("pt", [128, 128], BF16)
        PT2 = psf("pt2", [128, 128], BF16)
        PA = psf("pa", [128, 128], F32)
        PB = psf("pb", [128, 128], F32)
        PF0 = psf("pf0", [128, 512], F32)
        PF1 = psf("pf1", [128, 512], F32)
        PX = psf("px", [96, 128], F32)
        PD = psf("pd", [128, 128], F32)

        sems = {e: ctx.enter_context(nc.semaphore(f"sem_{e}")) for e in ENGINES}
        block = ctx.enter_context(nc.Block())

        col = lambda tile, i: tile[:, i:i + 1]
        sq_ = S.seq

        def dma(dst, src, after=(), chain=True):
            return sq_("sync", lambda e, d=dst, s=src: e.dma_start(out=d, in_=s),
                       after=after, chain=chain)

        def V(fn, after=(), lag=0, chain=True):
            return sq_("vector", fn, after=after, lag=lag, chain=chain)

        def A(fn, after=(), lag=0, chain=True):
            return sq_("scalar", fn, after=after, lag=lag, chain=chain)

        def T(fn, after=(), chain=True, sig=False):
            return sq_("tensor", fn, after=after, chain=chain, sig=sig)

        def G(fn, after=(), lag=0, chain=True):
            return sq_("gpsimd", fn, after=after, lag=lag, chain=chain)

        def rne_s(src, t1, t2, out, msk=None):
            m = msk if msk is not None else MSKA[:]
            V(lambda e: e.tensor_scalar(out, src, P23, P23, ALU.add, ALU.subtract))
            V(lambda e: e.tensor_scalar(t2, out, P52, P52, ALU.add, ALU.subtract))
            V(lambda e: e.tensor_scalar(m, t2, 0.0, None, ALU.not_equal))
            return V(lambda e: e.copy_predicated(out, m, t2))

        def exact_pow2(buf):
            return V(lambda e: e.tensor_scalar(buf, buf, P23, P23,
                                               ALU.add, ALU.subtract))

        def f32z_s(w32, t2, msk=None):
            m = msk if msk is not None else MSKA[:]
            V(lambda e: e.tensor_scalar(t2, w32, P52, P52, ALU.add, ALU.subtract))
            V(lambda e: e.tensor_scalar(m, t2, 0.0, None, ALU.not_equal))
            return V(lambda e: e.copy_predicated(w32, m, t2))

        G(lambda e: e.memset(ONE1[:], 1.0))
        G(lambda e: e.memset(ONEC[:], 1.0))
        G(lambda e: e.memset(IDT[:], 0.0))
        G(lambda e: e.affine_select(
            out=IDT[:], in_=IDT[:], compare_op=ALU.not_equal, fill=1.0,
            base=0, pattern=[[-1, 128]], channel_multiplier=1))
        for cc, val in [(C_EPS, 1e-5), (C_ONE, 1.0), (C_NEG1, -1.0),
                        (C_mLN2, -LN2), (C_LN2, LN2), (C_7LN2, 7 * LN2),
                        (C_m7LN2, -7 * LN2), (C_15LN2, 15 * LN2),
                        (C_8LN2, 8 * LN2), (C_6LN2, 6 * LN2)]:
            G(lambda e, c=cc, v=val: e.memset(col(SC, c), v))

        def bcast11(src11, dst_col):
            T(lambda e, s=src11: e.matmul(PF0[:, 0:1], ONE1[:], s,
                                          start=True, stop=True))
            V(lambda e, d=dst_col: e.tensor_copy(d, PF0[:, 0:1]))

        def xpart_sum(col_ap, npart, dst11):
            T(lambda e, c=col_ap, n=npart: e.matmul(
                PD[0:1, 0:1], c, ONEC[0:n, :], start=True, stop=True))
            V(lambda e, d=dst11: e.tensor_copy(d, PD[0:1, 0:1]))

        dma(T11[0:1, 0:1], ss_h[:])
        dma(T11[0:1, 1:2], s_in_h[:])
        dma(T11[0:1, 2:3], s_k_h[:])
        V(lambda e: e.tensor_tensor(T11[0:1, 3:4], T11[0:1, 1:2], T11[0:1, 2:3], ALU.mult))
        V(lambda e: e.reciprocal(T11[0:1, 4:5], T11[0:1, 1:2]))
        V(lambda e: e.reciprocal(T11[0:1, 5:6], T11[0:1, 2:3]))
        bcast11(T11[0:1, 0:1], col(SC, C_SS))
        bcast11(T11[0:1, 3:4], col(SC, C_CFFN))
        bcast11(T11[0:1, 4:5], col(SC, C_RSIN))
        bcast11(T11[0:1, 5:6], col(SC, C_RSK))

        dma(GROW, gamma_h[:])
        for h in range(2):
            T(lambda e, h=h: e.matmul(PF0[:, 0:512], ONE1[:],
                                      SA[0:1, h * 512:(h + 1) * 512],
                                      start=True, stop=True))
            V(lambda e, h=h: e.tensor_copy(GAM[:, h * 512:(h + 1) * 512], PF0[:, 0:512]))
        dma(FBROW, fbias_h[:])
        V(lambda e: e.tensor_copy(FBROWF, FBROW))
        for h in range(2):
            T(lambda e, h=h: e.matmul(PF0[:, 0:512], ONE1[:],
                                      SB_[0:1, h * 512:(h + 1) * 512],
                                      start=True, stop=True))
            V(lambda e, h=h: e.tensor_copy(FBB[:, h * 512:(h + 1) * 512], PF0[:, 0:512]))

        for c in range(16):
            dma(CW[:, c * 4:(c + 1) * 4], conv_w_h[c * 128:(c + 1) * 128, :])
            dma(col(CBIA, c), conv_b_h[c * 128:(c + 1) * 128, :])
            dma(col(BDT, c), b_dt_h[c * 128:(c + 1) * 128, :])

        def quantize_weight(dram, nchunks, width, rsw_col, s_dst_col, rs_dst_col,
                            nelem, chunkbuf, sgnbuf, mskbuf, qbuf_bf16,
                            after_quant_chunk):
            for c in range(nchunks):
                dma(chunkbuf[:, 0:width], dram[c * 128:(c + 1) * 128, :])
                V(lambda e: e.reduce_sum(col(COLS, 0), chunkbuf[:, 0:width],
                                         AXX, apply_absolute_value=True))
                if c == 0:
                    V(lambda e: e.tensor_copy(col(COLS, 1), col(COLS, 0)))
                else:
                    V(lambda e: e.tensor_tensor(col(COLS, 1), col(COLS, 1),
                                                col(COLS, 0), ALU.add))
            xpart_sum(col(COLS, 1), 128, T11[0:1, 6:7])
            V(lambda e, n=nelem: e.tensor_scalar(T11[0:1, 6:7], T11[0:1, 6:7],
                                                 1.0 / n, 1e-8, ALU.mult, ALU.add))
            V(lambda e: e.reciprocal(T11[0:1, 7:8], T11[0:1, 6:7]))
            bcast11(T11[0:1, 6:7], s_dst_col)
            bcast11(T11[0:1, 7:8], rs_dst_col)
            for c in range(nchunks):
                dma(chunkbuf[:, 0:width], dram[c * 128:(c + 1) * 128, :])
                A(lambda e: e.activation(sgnbuf[:, 0:width], chunkbuf[:, 0:width], AF.Sign))
                V(lambda e, rc=rsw_col: e.tensor_scalar(
                    mskbuf[:, 0:width], chunkbuf[:, 0:width], rc, None, ALU.mult))
                A(lambda e: e.activation(mskbuf[:, 0:width], mskbuf[:, 0:width], AF.Abs))
                V(lambda e: e.tensor_scalar(mskbuf[:, 0:width], mskbuf[:, 0:width],
                                            0.5, None, ALU.is_gt))
                V(lambda e: e.tensor_tensor(qbuf_bf16[:, 0:width], sgnbuf[:, 0:width],
                                            mskbuf[:, 0:width], ALU.mult))
                after_quant_chunk(c)

        def win_chunk(c):
            for k in range(NKD):
                T(lambda e, k=k: e.transpose(PT[:], XNB[:, k * 128:(k + 1) * 128], IDT[:]))
                V(lambda e, k=k, c=c: e.tensor_copy(
                    WIN[k][:, c * 128:(c + 1) * 128], PT[:]))
        quantize_weight(w_in_h, 32, D, col(SC, C_RSW), col(SC, C_SWIN),
                        col(SC, C_RSW), 4096 * 1024, XT, SA, SB_, XNB, win_chunk)

        def wout_chunk(c):
            for k in range(NCH):
                T(lambda e, k=k: e.transpose(PT[:], XCB[:, k * 128:(k + 1) * 128], IDT[:]))
                V(lambda e, k=k, c=c: e.tensor_copy(
                    WOUT[k][:, c * 128:(c + 1) * 128], PT[:]))
        quantize_weight(w_out_h, 8, DI, col(SC, C_RSWOUT), col(SC, C_SWOUT),
                        col(SC, C_RSWOUT), 1024 * 2048, Z, XCP[:, 0:DI],
                        XC[:, 0:DI], XCB, wout_chunk)

        def wdt_chunk(c):
            T(lambda e: e.transpose(PT[0:64, :], XNB[:, 0:64], IDT[:]))
            V(lambda e, c=c: e.tensor_copy(WDT[:, c * 128:(c + 1) * 128], PT[0:64, :]))
        quantize_weight(w_dt_h, 16, 64, col(SC, C_RSDT), col(SC, C_SDT),
                        col(SC, C_RSDT), 2048 * 64, XT, SA, SB_, XNB, wdt_chunk)

        dma(XCP[0:96, 0:DI], w_x_h[:])
        V(lambda e: e.reduce_sum(COLS[0:96, 0:1], XCP[0:96, 0:DI], AXX,
                                 apply_absolute_value=True))
        xpart_sum(COLS[0:96, 0:1], 96, T11[0:1, 6:7])
        V(lambda e: e.tensor_scalar(T11[0:1, 6:7], T11[0:1, 6:7],
                                    1.0 / (96 * 2048), 1e-8, ALU.mult, ALU.add))
        V(lambda e: e.reciprocal(T11[0:1, 7:8], T11[0:1, 6:7]))
        bcast11(T11[0:1, 6:7], col(SC, C_SX))
        bcast11(T11[0:1, 7:8], col(SC, C_RSX))
        A(lambda e: e.activation(Z[0:96, 0:DI], XCP[0:96, 0:DI], AF.Sign))
        V(lambda e: e.tensor_scalar(XC[0:96, 0:DI], XCP[0:96, 0:DI],
                                    SC[0:96, C_RSX:C_RSX + 1], None, ALU.mult))
        A(lambda e: e.activation(XC[0:96, 0:DI], XC[0:96, 0:DI], AF.Abs))
        V(lambda e: e.tensor_scalar(XC[0:96, 0:DI], XC[0:96, 0:DI], 0.5, None, ALU.is_gt))
        V(lambda e: e.tensor_tensor(XCB[0:96, 0:DI], Z[0:96, 0:DI],
                                    XC[0:96, 0:DI], ALU.mult))
        for c in range(16):
            T(lambda e, c=c: e.transpose(PT[:, 0:96], XCB[0:96, c * 128:(c + 1) * 128],
                                         IDT[0:96, 0:96]))
            V(lambda e, c=c: e.tensor_copy(WXT[:, c * 96:(c + 1) * 96], PT[:, 0:96]))

        def quant_fp8(vbuf, outbuf_bf16):
            A(lambda e: e.activation(SD[:], vbuf, AF.Abs))
            V(lambda e: e.tensor_scalar_max(SD[:], SD[:], 1e-20))
            A(lambda e: e.activation(SE, SD[:], AF.Ln))
            V(lambda e: e.tensor_scalar(SE, SE, RLN2, 255.5, ALU.mult, ALU.add))
            rne_s(SE, None, SG[:], SE)
            V(lambda e: e.tensor_scalar(SE, SE, -256.0, -6.0, ALU.add, ALU.max))
            V(lambda e: e.tensor_scalar_min(SE, SE, 8.0))
            A(lambda e: e.activation(SD[:], SE, AF.Exp,
                                     bias=col(SC, C_8LN2), scale=col(SC, C_mLN2)))
            exact_pow2(SD[:])
            V(lambda e: e.tensor_tensor(SG[:], vbuf, SD[:], ALU.mult))
            V(lambda e: e.tensor_scalar(SG[:], SG[:], float(2.0 ** 15), None, ALU.mult))
            f32z_s(SG[:], SRES[:])
            V(lambda e: e.tensor_scalar(GBUF[:], SG[:], float(2.0 ** -20), 1024.0,
                                        ALU.mult, ALU.add))
            rne_s(GBUF[:], None, SRES[:], GBUF[:])
            A(lambda e: e.activation(SD[:], SE, AF.Exp,
                                     bias=col(SC, C_6LN2), scale=col(SC, C_LN2)))
            exact_pow2(SD[:])
            V(lambda e: e.tensor_scalar(SD[:], SD[:], float(2.0 ** -9), None, ALU.mult))
            V(lambda e: e.scalar_tensor_tensor(SRES[:], GBUF[:], -1024.0, SD[:],
                                               ALU.add, ALU.mult))
            V(lambda e: e.tensor_copy(outbuf_bf16, SRES[:]))

        for k in range(NKD):
            dma(XQB[:], kern_h[k * 128:(k + 1) * 128, :])
            V(lambda e: e.tensor_copy(SA[:], XQB[:]))
            V(lambda e: e.tensor_scalar(SA[:], SA[:], col(SC, C_RSK), None, ALU.mult))
            quant_fp8(SA[:], XQB[:])
            V(lambda e, k=k: e.tensor_copy(KQ[k][:], XQB[:]))

        def soft_bf16_x2(src, dst):
            A(lambda e: e.activation(SA[:], src, AF.Abs))
            V(lambda e: e.tensor_scalar_max(SA[:], SA[:], 1e-20))
            A(lambda e: e.activation(SB_[:], SA[:], AF.Ln))
            V(lambda e: e.tensor_scalar(SB_[:], SB_[:], RLN2, 255.5, ALU.mult, ALU.add))
            rne_s(SB_[:], None, SA[:], SE)
            V(lambda e: e.tensor_scalar(SE, SE, -256.0, -15.0, ALU.add, ALU.max))
            V(lambda e: e.tensor_scalar_min(SE, SE, 6.0))
            A(lambda e: e.activation(SA[:], SE, AF.Exp,
                                     bias=col(SC, C_mLN2), scale=col(SC, C_mLN2)))
            exact_pow2(SA[:])
            A(lambda e: e.activation(SB_[:], SE, AF.Exp,
                                     bias=col(SC, C_7LN2), scale=col(SC, C_mLN2)))
            exact_pow2(SB_[:])
            V(lambda e: e.tensor_scalar(MSKA[:], SE, -7.5, None, ALU.is_ge))
            V(lambda e: e.tensor_scalar(SA[:], SA[:], 256.0, None, ALU.mult))
            V(lambda e: e.select(SD[:], MSKA[:], SB_[:], SA[:]))
            V(lambda e: e.tensor_tensor(Q1, src, SD[:], ALU.mult))
            V(lambda e: e.tensor_scalar(Q1, Q1, 65536.0, None, ALU.mult))
            f32z_s(Q1, Q2)
            V(lambda e: e.tensor_scalar(Q3, Q1, float(2.0 ** -16), 1024.0,
                                        ALU.mult, ALU.add))
            rne_s(Q3, None, Q2, Q3)
            A(lambda e: e.activation(SA[:], SE, AF.Exp,
                                     bias=col(SC, C_15LN2), scale=col(SC, C_LN2)))
            exact_pow2(SA[:])
            V(lambda e: e.tensor_scalar(SA[:], SA[:], float(2.0 ** -22), None, ALU.mult))
            A(lambda e: e.activation(SB_[:], SE, AF.Exp,
                                     bias=col(SC, C_7LN2), scale=col(SC, C_LN2)))
            exact_pow2(SB_[:])
            V(lambda e: e.tensor_scalar(SB_[:], SB_[:], float(2.0 ** -14), None, ALU.mult))
            V(lambda e: e.tensor_scalar(MSKA[:], SE, -1.5, None, ALU.is_ge))
            V(lambda e: e.select(SD[:], MSKA[:], SB_[:], SA[:]))
            V(lambda e: e.scalar_tensor_tensor(dst, Q3, -1024.0, SD[:],
                                               ALU.add, ALU.mult))

        def soft_bf16_wide(src, dst, ta, tb, tc, td):
            A(lambda e: e.activation(ta, src, AF.Abs))
            V(lambda e: e.tensor_scalar_max(ta, ta, 1e-30))
            A(lambda e: e.activation(tb, ta, AF.Ln))
            V(lambda e: e.tensor_scalar(tb, tb, RLN2, 255.5, ALU.mult, ALU.add))
            rne_s(tb, None, tc, tb)
            V(lambda e: e.tensor_scalar(tb, tb, -256.0, -30.0, ALU.add, ALU.max))
            V(lambda e: e.tensor_scalar_min(tb, tb, 12.0))
            A(lambda e: e.activation(ta, tb, AF.Exp,
                                     bias=col(SC, C_7LN2), scale=col(SC, C_mLN2)))
            V(lambda e: e.tensor_tensor(tc, src, ta, ALU.mult))
            V(lambda e: e.tensor_scalar(tc, tc, 65536.0, None, ALU.mult))
            f32z_s(tc, td)
            V(lambda e: e.tensor_scalar(td, tc, float(2.0 ** -16), 1024.0,
                                        ALU.mult, ALU.add))
            rne_s(td, None, tc, td)
            A(lambda e: e.activation(ta, tb, AF.Exp,
                                     bias=col(SC, C_m7LN2), scale=col(SC, C_LN2)))
            V(lambda e: e.scalar_tensor_tensor(dst, td, -1024.0, ta,
                                               ALU.add, ALU.mult))

        def sp_half(xin, dst, ta, tb, tc):
            A(lambda e: e.activation(ta, xin, AF.Square))
            V(lambda e: e.tensor_scalar(ta, ta, 4.0, 1e6, ALU.add, ALU.min))
            V(lambda e: e.tensor_scalar(tb, ta, 16.0, -0.375, ALU.is_gt, ALU.mult))
            V(lambda e: e.tensor_scalar(tc, ta, 64.0, -0.0625, ALU.is_gt, ALU.mult))
            V(lambda e: e.scalar_tensor_tensor(tb, tb, 0.5, tc, ALU.add, ALU.add))
            V(lambda e: e.tensor_scalar(tc, ta, 256.0, -0.03125, ALU.is_gt, ALU.mult))
            V(lambda e: e.tensor_tensor(tb, tb, tc, ALU.add))
            for it in range(3):
                V(lambda e: e.tensor_tensor(tc, tb, tb, ALU.mult))
                V(lambda e: e.scalar_tensor_tensor(tc, tc, -0.5, ta, ALU.mult, ALU.mult))
                V(lambda e: e.scalar_tensor_tensor(tb, tc, 1.5, tb, ALU.add, ALU.mult))
                if it == 0:
                    V(lambda e: e.tensor_scalar_max(tb, tb, 1e-6))
            V(lambda e: e.scalar_tensor_tensor(tc, ta, 0.5, tb, ALU.mult, ALU.mult))
            V(lambda e: e.scalar_tensor_tensor(dst, xin, 0.5, tc, ALU.mult, ALU.add))

        vec_now = S.now("vector")
        ev_tok = [vec_now, vec_now]
        war_prev = S.now("scalar")

        xcp3 = XCP[:].rearrange("p (c w) -> p c w", w=131)
        halo3 = HALO[:].rearrange("p (c w) -> p c w", w=3)

        for tile in range(n_tiles):
            seq_start = (tile % seq_tiles) == 0
            r0 = tile * TOK

            dma(XT[:], x_h[r0:r0 + TOK, :])

            A(lambda e: e.activation(SA[:], XT[:], AF.Square, accum_out=col(COLS, 0)))
            A(lambda e: e.activation(SA[:], XT[:], AF.Identity, accum_out=col(COLS, 1)))
            V(lambda e: e.tensor_scalar(col(COLS, 2), col(COLS, 1), 1.0 / D, None, ALU.mult))
            V(lambda e: e.tensor_scalar(col(COLS, 3), col(COLS, 0), 1.0 / D, None, ALU.mult))
            V(lambda e: e.tensor_tensor(col(COLS, 4), col(COLS, 2), col(COLS, 2), ALU.mult))
            V(lambda e: e.tensor_tensor(col(COLS, 5), col(COLS, 3), col(COLS, 4), ALU.subtract))
            A(lambda e: e.activation(col(COLS, 6), col(COLS, 5), AF.Sqrt,
                                     bias=col(SC, C_EPS)))
            V(lambda e: e.reciprocal(col(COLS, 7), col(COLS, 6)))
            V(lambda e: e.tensor_scalar(XN[:], XT[:], col(COLS, 2), col(COLS, 7),
                                        ALU.subtract, ALU.mult))

            A(lambda e: e.activation(SA[:], XN[:], AF.Identity, accum_out=col(COLS, 8)))
            V(lambda e: e.tensor_scalar(col(COLS, 9), col(COLS, 8), 1.0 / D, None, ALU.mult))
            V(lambda e: e.tensor_scalar(SA[:], XN[:], col(COLS, 9), None, ALU.subtract))
            A(lambda e: e.activation(SB_[:], SA[:], AF.Square, accum_out=col(COLS, 10)))
            V(lambda e: e.tensor_scalar(col(COLS, 15), col(COLS, 10), 1.0 / D, 1e-9,
                                        ALU.mult, ALU.add))
            for i, (thr, dlt) in enumerate(
                    [(0.0625, -2.0), (0.25, -1.0), (4.0, -0.5), (16.0, -0.25),
                     (64.0, -0.125), (256.0, -0.0625), (1024.0, -0.03125),
                     (4096.0, -0.015625), (16384.0, -0.0078125),
                     (65536.0, -0.00390625)]):
                V(lambda e, i=i, t_=thr, d_=dlt: e.tensor_scalar(
                    col(COLS, 4 + i), col(COLS, 15), t_, d_, ALU.is_ge, ALU.mult),
                  lag=i)
            V(lambda e: e.reduce_sum(col(COLS, 14), COLS[:, 4:14], AXX))
            V(lambda e: e.tensor_scalar(col(COLS, 14), col(COLS, 14), 4.0, None, ALU.add))
            V(lambda e: e.tensor_tensor(col(COLS, 14), col(COLS, 14),
                                        col(SC, C_SS), ALU.mult))
            V(lambda e: e.tensor_scalar(SA[:], SA[:], col(COLS, 14), None, ALU.mult))
            V(lambda e: e.tensor_tensor(XNB[:], SA[:], GAM[:], ALU.mult))

            soft_bf16_x2(XN[:], SG[:])
            if debug and tile == 0:
                dma(dbg["xn"], XN[:])
                dma(dbg["x2"], SG[:])
            V(lambda e: e.tensor_scalar(SA[:], SG[:], col(SC, C_RSIN), None, ALU.mult))
            quant_fp8(SA[:], XQB[:])
            if debug and tile == 0:
                dma(dbg["xqb"], XQB[:])

            vcur = S.now("vector")
            for idx in range(16):
                srcb, dstb = (XNB, XNB_FM) if idx < 8 else (XQB, XQ_FM)
                k = idx % 8
                pt = PT if idx % 2 == 0 else PT2
                ttok = T(lambda e, k=k, s=srcb, p=pt: e.transpose(
                    p[:], s[:, k * 128:(k + 1) * 128], IDT[:]),
                    after=[ev_tok[idx % 2], vcur], chain=False, sig=True)
                ev_tok[idx % 2] = A(lambda e, k=k, d=dstb, p=pt: e.activation(
                    d[:, k * 128:(k + 1) * 128], p[:], AF.Identity),
                    after=[ttok], chain=False)
            S.last_eng = "vector"

            ffn_tok = []
            for h, PF in ((0, PF0), (1, PF1)):
                tk = None
                for k in range(NKD):
                    tk = T(lambda e, k=k, h=h, PF=PF: e.matmul(
                        PF[:], XQ_FM[:, k * 128:(k + 1) * 128],
                        KQ[k][:, h * 512:(h + 1) * 512],
                        start=(k == 0), stop=(k == NKD - 1)),
                        after=([ev_tok[0], ev_tok[1]] if k == 0 else []),
                        chain=False, sig=(k == NKD - 1))
                ffn_tok.append(tk)
            if seq_start:
                V(lambda e: e.memset(xcp3[:, :, 0:3], 0.0))
                hal_tok = V(lambda e: e.memset(STATE[:], 0.0))
            else:
                hal_tok = V(lambda e: e.tensor_copy(xcp3[:, :, 0:3], halo3[:, :, :]))
            ev_win = [None, None]
            for m in range(32):
                PM = PA if (m % 2 == 0) else PB
                aft = []
                if m == 0:
                    aft = [hal_tok, war_prev]
                elif ev_win[m % 2] is not None:
                    aft = [ev_win[m % 2]]
                tk = None
                for k in range(NKD):
                    tk = T(lambda e, m=m, k=k, PM=PM: e.matmul(
                        PM[:], WIN[k][:, m * 128:(m + 1) * 128],
                        XNB_FM[:, k * 128:(k + 1) * 128],
                        start=(k == 0), stop=(k == NKD - 1)),
                        after=(aft if k == 0 else []), chain=False,
                        sig=(k == NKD - 1))
                if m < 16:
                    ev_win[m % 2] = V(lambda e, m=m, PM=PM: e.tensor_scalar(
                        XCP[:, m * 131 + 3:m * 131 + 131], PM[:],
                        col(SC, C_SWIN), None, ALU.mult),
                        after=[tk], chain=False)
                else:
                    ev_win[m % 2] = V(lambda e, m=m, PM=PM: e.tensor_scalar(
                        Z[:, (m - 16) * 128:(m - 15) * 128], PM[:],
                        col(SC, C_SWIN), None, ALU.mult),
                        after=[tk], chain=False)
            S.last_eng = "vector"
            V(lambda e: e.tensor_copy(halo3[:, :, :], xcp3[:, :, 128:131]))
            for h, PF in ((0, PF0), (1, PF1)):
                V(lambda e, h=h, PF=PF: e.tensor_scalar(
                    SA[:, h * 512:(h + 1) * 512], PF[:], col(SC, C_CFFN), None, ALU.mult),
                    after=[ffn_tok[h]], chain=False)
            S.last_eng = "vector"
            soft_bf16_wide(SA[:], SD[:], SB_[:], SG[:], GBUF[:], SRES[:])
            V(lambda e: e.tensor_tensor(GBUF[:], SD[:], FBB[:], ALU.add))
            A(lambda e: e.activation(SG[:], GBUF[:], AF.Silu))
            if debug and tile == 0:
                dma(dbg["g"], SG[:])
            if debug and tile == 0:
                dma(dbg["xcp"], XCP[:])

            for h in range(2):
                for j in range(8):
                    m = h * 8 + j
                    base = m * 131
                    V(lambda e, b=base, m=m: e.tensor_scalar(
                        T1[:], XCP[:, b:b + 128], col(CW, 4 * m), None, ALU.mult))
                    V(lambda e, b=base, m=m: e.scalar_tensor_tensor(
                        T2[:], XCP[:, b + 1:b + 129], col(CW, 4 * m + 1), T1[:],
                        ALU.mult, ALU.add))
                    V(lambda e, b=base, m=m: e.tensor_scalar(
                        T1[:], XCP[:, b + 3:b + 131], col(CW, 4 * m + 3), col(CBIA, m),
                        ALU.mult, ALU.add))
                    V(lambda e, b=base, m=m: e.scalar_tensor_tensor(
                        T1[:], XCP[:, b + 2:b + 130], col(CW, 4 * m + 2), T1[:],
                        ALU.mult, ALU.add))
                    V(lambda e, j=j: e.tensor_tensor(
                        SA[:, j * 128:(j + 1) * 128], T2[:], T1[:], ALU.add))
                if debug and tile == 0 and h == 0:
                    dma(dbg["convo"], SA[:])
                sp_half(SA[:], XC[:, h * D:(h + 1) * D], SB_[:], SD[:], GBUF[:])
            xcb_tok = V(lambda e: e.tensor_copy(XCB[:], XC[:]))
            if debug and tile == 0:
                dma(dbg["xc"], XC[:])
                dma(dbg["z"], Z[:])

            wx_tok = None
            for c in range(16):
                wx_tok = T(lambda e, c=c: e.matmul(
                    PX[:], WXT[:, c * 96:(c + 1) * 96], XCB[:, c * 128:(c + 1) * 128],
                    start=(c == 0), stop=(c == 15)),
                    after=([xcb_tok] if c == 0 else []), chain=False, sig=(c == 15))
            for h in range(2):
                zh = Z[:, h * D:(h + 1) * D]
                sp_half(zh, zh, SB_[:], SD[:], GBUF[:])
            dlb_tok = V(lambda e: e.tensor_scalar(
                DLB[:], PX[0:64, :], SC[0:64, C_SX:C_SX + 1], None, ALU.mult),
                after=[wx_tok], chain=False)
            S.last_eng = "vector"

            ev_dt = [dlb_tok, dlb_tok]
            for m in range(16):
                PM = PD if (m % 2 == 0) else PA
                tk = T(lambda e, m=m, PM=PM: e.matmul(
                    PM[:], WDT[:, m * 128:(m + 1) * 128], DLB[:],
                    start=True, stop=True), after=[ev_dt[m % 2]],
                    chain=False, sig=True)
                ev_dt[m % 2] = A(lambda e, m=m, PM=PM: e.activation(
                    DTBUF[:, m * 128:(m + 1) * 128], PM[:], AF.Identity,
                    bias=col(BDT, m), scale=col(SC, C_SDT)),
                    after=[tk], chain=False)
            S.last_eng = "scalar"
            war_prev = S.now("scalar")

            ylast = None
            for h in range(2):
                dth = DTBUF[:, h * D:(h + 1) * D]
                zh = Z[:, h * D:(h + 1) * D]
                A(lambda e, dth=dth: e.activation(SA[:], dth, AF.Abs))
                A(lambda e: e.activation(SB_[:], SA[:], AF.Ln, bias=col(SC, C_ONE)))
                A(lambda e: e.activation(SD[:], SB_[:], AF.Exp, scale=col(SC, C_NEG1)))
                V(lambda e: e.scalar_tensor_tensor(SB_[:], SA[:], 1.0, SD[:],
                                                   ALU.add, ALU.mult))
                V(lambda e: e.tensor_scalar(SB_[:], SB_[:], -1.0, 2.0, ALU.mult, ALU.add))
                V(lambda e: e.tensor_tensor(SD[:], SD[:], SB_[:], ALU.mult))
                V(lambda e, dth=dth: e.tensor_tensor(SA[:], dth, SD[:], ALU.mult))
                V(lambda e: e.tensor_scalar(SA[:], SA[:], 0.5, 0.5, ALU.mult, ALU.add))
                V(lambda e: e.tensor_scalar(SA[:], SA[:], 32768.0, 2.0, ALU.mult, ALU.add))
                rne_s(SA[:], None, GBUF[:], SA[:])
                V(lambda e: e.tensor_scalar(SB_[:], SA[:], float(2.0 ** -15),
                                            float(-2.0 ** -14), ALU.mult, ALU.add))
                V(lambda e: e.tensor_scalar(GBUF[:], SA[:], float(-2.0 ** -15),
                                            1.00006103515625, ALU.mult, ALU.add))
                V(lambda e, h=h: e.tensor_tensor(SD[:], GBUF[:],
                                                 XC[:, h * D:(h + 1) * D], ALU.mult))
                for j in range(8):
                    m = h * 8 + j
                    V(lambda e, j=j, m=m: e.tensor_tensor_scan(
                        GBUF[:, j * 128:(j + 1) * 128],
                        SB_[:, j * 128:(j + 1) * 128],
                        SD[:, j * 128:(j + 1) * 128],
                        col(STATE, m), ALU.mult, ALU.add), lag=j)
                for j in range(8):
                    m = h * 8 + j
                    V(lambda e, j=j, m=m: e.tensor_copy(
                        col(STATE, m), GBUF[:, j * 128 + 127:j * 128 + 128]), lag=7)
                for j in range(8):
                    m = h * 8 + j
                    ylast = V(lambda e, j=j, m=m, zh=zh: e.tensor_tensor(
                        Y[:, m * 128:(m + 1) * 128], GBUF[:, j * 128:(j + 1) * 128],
                        zh[:, j * 128:(j + 1) * 128], ALU.mult), lag=7)
            if debug and tile == 0:
                dma(dbg["y"], Y[:])

            wo_tok = []
            for h, PF in ((0, PF0), (1, PF1)):
                tk = None
                for k in range(NCH):
                    tk = T(lambda e, k=k, h=h, PF=PF: e.matmul(
                        PF[:], Y[:, k * 128:(k + 1) * 128],
                        WOUT[k][:, h * 512:(h + 1) * 512],
                        start=(k == 0), stop=(k == NCH - 1)),
                        after=([ylast] if k == 0 else []),
                        chain=False, sig=(k == NCH - 1))
                wo_tok.append(tk)
            for h, PF in ((0, PF0), (1, PF1)):
                V(lambda e, h=h, PF=PF: e.scalar_tensor_tensor(
                    SRES[:, h * 512:(h + 1) * 512], PF[:], col(SC, C_SWOUT),
                    XN[:, h * 512:(h + 1) * 512], ALU.mult, ALU.add),
                    after=[wo_tok[h]], chain=False)
            S.last_eng = "vector"
            if debug and tile == 0:
                dma(dbg["sres"], SRES[:])

            V(lambda e: e.tensor_tensor(SA[:], SRES[:], SG[:], ALU.mult))
            V(lambda e: e.tensor_tensor(SA[:], SA[:], XT[:], ALU.add))
            dma(out_h[r0:r0 + TOK, :], SA[:])

        G(lambda e: e.engine_nop())

        @block.sync
        def _(sync):
            S.emit(sync, "sync", sems)

        @block.tensor
        def _(tensor):
            S.emit(tensor, "tensor", sems)

        @block.vector
        def _(vector):
            S.emit(vector, "vector", sems)

        @block.scalar
        def _(scalar):
            S.emit(scalar, "scalar", sems)

        @block.gpsimd
        def _(gpsimd):
            S.emit(gpsimd, "gpsimd", sems)

    return nc


_NC_CACHE = {}


def _get_nc(n_tiles, seq_tiles):
    key = (n_tiles, seq_tiles)
    if key not in _NC_CACHE:
        _NC_CACHE[key] = build_nc(n_tiles, seq_tiles)
    return _NC_CACHE[key]


def _in_map(x_shard, inputs):
    f32 = lambda a: np.ascontiguousarray(np.asarray(a), dtype=np.float32)
    return {
        "x": f32(x_shard).reshape(-1, D),
        "gamma": f32(inputs["L0_attn_gamma"]).reshape(1, D),
        "step_size": f32(inputs["L0_attn_step_size"]).reshape(1, 1),
        "w_in": f32(inputs["L0_attn_W_in"]),
        "conv_w": f32(inputs["L0_attn_conv_w"]).reshape(DI, 4),
        "conv_b": f32(inputs["L0_attn_conv_b"]).reshape(DI, 1),
        "w_x": f32(inputs["L0_attn_W_x"]),
        "w_dt": f32(inputs["L0_attn_W_dt"]),
        "b_dt": f32(inputs["L0_attn_b_dt"]).reshape(DI, 1),
        "w_out": f32(inputs["L0_attn_W_out"]),
        "ffn_kernel": np.ascontiguousarray(
            np.asarray(inputs["L0_ffn_kernel"]), dtype=ml_dtypes.bfloat16),
        "ffn_input_scale": f32(inputs["L0_ffn_input_scale"]).reshape(1, 1),
        "ffn_kernel_scale": f32(inputs["L0_ffn_kernel_scale"]).reshape(1, 1),
        "ffn_bias": np.ascontiguousarray(
            np.asarray(inputs["L0_ffn_bias"]), dtype=ml_dtypes.bfloat16).reshape(1, D),
    }


def kernel(**inputs):
    x = np.asarray(inputs["L0_attn_x"])
    B, L, _ = x.shape
    per = B // N_CORES
    nc = _get_nc((per * L) // TOK, L // TOK)
    in_maps = [_in_map(x[i * per:(i + 1) * per], inputs) for i in range(N_CORES)]
    res = run_bass_kernel_spmd(nc, in_maps, core_ids=list(range(N_CORES)))
    outs = [np.asarray(r["out"], dtype=np.float32).reshape(per, L, D)
            for r in res.results]
    return np.concatenate(outs, axis=0)
